# version 05
# baseline (speedup 1.0000x reference)
import math
import contextlib
import numpy as np
import concourse.bass as bass
import bass_rust
import concourse.mybir as mybir
from concourse.bass_utils import run_bass_kernel_spmd

F32 = mybir.dt.float32
A = mybir.AluOpType
AF = mybir.ActivationFunctionType

NCORES = 8
B, S, D = 4, 8192, 1024
S_CORE = S // NCORES
T_CORE = B * S_CORE
NT = T_CORE // 128
H, DH = 16, 64
LN_EPS = 1e-5
RMS_EPS = 1e-6
DEN_EPS = 1e-6
TINY = 1e-30
TH_A = [0.25, 0.75, 1.25, 1.75]
TH_B = [2.5, 3.5]
TH_C = [5.0]
LN2H = math.log(2.0) / 2.0

ENGINES = ("sync", "vector", "scalar", "tensor", "gpsimd")


class _Drainer:

    _PASS = {"wait_ge", "drain"}

    def __init__(self, e):
        self._e = e
        self.raw = e

    def __getattr__(self, name):
        f = getattr(self._e, name)
        if name in self._PASS or not callable(f):
            return f
        def wrapper(*a, **k):
            r = f(*a, **k)
            self._e.drain()
            return r
        return wrapper


class Sem:
    def __init__(self, handle):
        self.h = handle
        self.n = 0


class Sched:

    def __init__(self, nc, stack):
        self.nc = nc
        self.q = {e: [] for e in ENGINES}
        self.sem = {}
        for name in ("d", "v", "s", "t", "g", "o", "c", "w"):
            self.sem[name] = Sem(stack.enter_context(nc.semaphore("sem_" + name)))

    def emit(self, eng, builder, waits=(), inc=None, n=1, amt=1):
        frozen = tuple((s, c) for (s, c) in waits if c > 0)

        def fn(e):
            for s, c in frozen:
                e.wait_ge(s.h, c)
            out = builder(e)
            if inc is not None:
                instrs = out if isinstance(out, (list, tuple)) else [out]
                assert len(instrs) == n, f"expected {n} instrs, got {len(instrs)}"
                for ins in instrs:
                    ins.then_inc(inc.h, amt)

        self.q[eng].append(fn)
        if inc is not None:
            inc.n += n * amt

    def v(self, builder, waits=()):
        self.emit("vector", builder, waits, self.sem["v"])

    def s(self, builder, waits=()):
        self.emit("scalar", builder, waits, self.sem["s"])

    def t(self, builder, waits=()):
        self.emit("tensor", builder, waits, self.sem["t"])

    def g(self, builder, waits=()):
        self.emit("gpsimd", builder, waits, self.sem["g"])

    def dma(self, builder, waits=(), n=1, sem="d"):
        self.emit("sync", builder, waits, self.sem[sem], n=n, amt=16)

    def gdma(self, builder, waits=(), n=1, sem="g"):
        self.emit("gpsimd", builder, waits, self.sem[sem], n=n, amt=16)

    def snap(self, *names):
        return [(self.sem[k], self.sem[k].n) for k in names]


def build_program(upto="all", nt=NT, tap=None):
    nc = bass.Bass("TRN2", target_bir_lowering=False, debug=False,
                   detect_race_conditions=False, dynamic_dma_scratch_size=512)

    def din(name, shape):
        return nc.dram_tensor(name, shape, F32, kind="ExternalInput").ap()

    x_in = din("x", [T_CORE, D])
    wqk_in = din("wqk_t", [D, 2 * D])
    bqk_in = din("bqk_pp", [128, 16])
    wv_in = din("wv_t", [D, D])
    bv_in = din("bv_row", [1, D])
    wpr_in = din("wproj_t", [D, D])
    bpr_in = din("bproj_row", [1, D])
    wffn_in = din("wffn", [D, D])
    dw_in = din("dw", [128, 8 * 32])
    bffn_in = din("bffn_row", [1, D])
    dact_in = din("dact", [128, 1])
    wqb_in = din("wqkv_b", [D, 3 * D])
    qnw_in = din("qnw_rep", [128, D])
    knw_in = din("knw_rep", [128, D])
    wout_in = din("wout", [D, D])
    bout_in = din("bout_row", [1, D])
    ident_in = din("ident", [128, 128])
    mask_in = din("mask", [128, 128])

    y_out = nc.dram_tensor("y", [T_CORE, D], F32, kind="ExternalOutput").ap()

    x2d = nc.dram_tensor("x2d", [T_CORE, D], F32).ap()
    qnd = nc.dram_tensor("qnd", [T_CORE, D], F32).ap()
    cc_in = nc.dram_tensor("cc_in", [64, 4160], F32)
    cc_out = nc.dram_tensor("cc_out", [64, 4160], F32)

    def sbt(name, shape, dt=F32):
        return nc.alloc_sbuf_tensor(name, shape, dt).ap()

    wqk = [sbt(f"wqk{c}", [128, 2 * D]) for c in range(8)]
    wqt = [sbt(f"wqt{c}", [128, D]) for c in range(8)]
    wbuf = [sbt(f"wbuf{i}", [128, D]) for i in range(3)]

    x_sb2 = [sbt(f"x_sb{p}", [128, D]) for p in range(2)]
    xln2b = [sbt(f"xln{p}", [128, D]) for p in range(2)]
    xlnT2 = [sbt(f"xlnT{p}", [128, D]) for p in range(2)]
    qkT = sbt("qkT", [128, 2 * D])
    qkOdd = sbt("qkOdd", [64, 2 * D])
    v_sb = sbt("v_sb", [128, D])
    p_sb = sbt("p_sb", [128, 2 * D])
    av_sb = sbt("av_sb", [128, D])
    avT = sbt("avT", [128, D])
    x1_sb = sbt("x1_sb", [128, D])
    uT = sbt("uT", [128, D])
    junk = sbt("junk", [128, D])
    xpb = uT
    qaxs = av_sb
    qrA = avT

    ident = sbt("ident_sb", [128, 128])
    mask = sbt("mask_sb", [128, 128])
    bqk_sb = sbt("bqk_sb", [128, 16])
    brows = sbt("brows", [1, 4 * D])
    bv_sb = brows[:, 0:D]
    bpr_sb = brows[:, D:2 * D]
    bo_sb = brows[:, 2 * D:3 * D]
    bffn_sb = brows[:, 3 * D:4 * D]
    qnw_sb = sbt("qnw_sb", [128, D])
    knw_sb = sbt("knw_sb", [128, D])
    ones1 = sbt("ones1", [1, 128])
    ones128 = sbt("ones128", [128, 1])

    q_am = sbt("q_am", [128, 32])
    q_m = sbt("q_m", [128, 32])
    q_p = sbt("q_p", [128, 32])
    q_t = sbt("q_t", [128, 32])
    q_msk = sbt("q_msk", [128, 32])
    q_tm = sbt("q_tm", [128, 32])
    q_base = sbt("q_base", [128, 32])
    q_rb = sbt("q_rb", [128, 32])
    q_f = sbt("q_f", [128, 32])
    w_t1 = sbt("w_t1", [128, 32])
    w_twod = sbt("w_twod", [128, 32])
    w_th = [sbt(f"w_th{i}", [128, 32]) for i in range(7)]

    l_sum = sbt("l_sum", [128, 1])
    l_ssq = sbt("l_ssq", [128, 1])
    l_m = sbt("l_m", [128, 1])
    l_t0 = sbt("l_t0", [128, 1])
    l_t1 = sbt("l_t1", [128, 1])
    l_var = sbt("l_var", [128, 1])
    l_r0 = sbt("l_r0", [128, 1])
    l_rstd = sbt("l_rstd", [128, 1])

    a_hm = sbt("a_hm", [128, 16])
    a_cm = sbt("a_cm", [128, 1])
    a_cmax = sbt("a_cmax", [128, 1])
    a_negc = sbt("a_negc", [128, 1])
    a_rs = sbt("a_rs", [128, 16])
    a_rr = sbt("a_rr", [128, 16])

    thA = [sbt(f"thA{i}", [128, 1]) for i in range(4)]
    thB = [sbt(f"thB{i}", [128, 1]) for i in range(2)]
    thC = [sbt(f"thC{i}", [128, 1]) for i in range(1)]
    half2d = sbt("half2d", [128, 1])
    d_w1 = sbt("d_w1", [128, 1])
    d_w2 = sbt("d_w2", [128, 1])
    c_ln2h = sbt("c_ln2h", [128, 1])
    c_four = sbt("c_four", [128, 1])
    c_0125 = sbt("c_0125", [128, 1])

    r_ssq = sbt("r_ssq", [128, 1])
    r_r0 = sbt("r_r0", [128, 1])
    r_t0 = sbt("r_t0", [128, 1])
    r_t1 = sbt("r_t1", [128, 1])
    r_rq = sbt("r_rq", [128, 1])

    def kvacc_ap(b):
        return wqk[b // 2][0:64, (b % 2) * 1024:(b % 2 + 1) * 1024]

    def ksacc_ap(b):
        return wqk[2][0:64, b * 16:(b + 1) * 16]

    def kvg_kv(b, h):
        col0 = b * 1024 + h * 64
        return wqk[3 + col0 // 2048][0:64, col0 % 2048:col0 % 2048 + 64]

    def kvg_ks(b, h):
        return wqk[5][0:64, b * 16 + h:b * 16 + h + 1]

    nrm_sb = sbt("nrm_sb", [128, 16])
    nrr_sb = sbt("nrr_sb", [128, 16])

    ps_big = nc.alloc_psum_tensor("ps_big", [128, 2 * D], F32).ap()
    ps_med = nc.alloc_psum_tensor("ps_med", [128, D], F32).ap()
    ps_aux = nc.alloc_psum_tensor("ps_aux", [128, D], F32).ap()

    stack = contextlib.ExitStack()
    sch = Sched(nc, stack)
    sd, sv, ssem, stn, sg = (sch.sem[k] for k in "dvstg")
    so, scc = sch.sem["o"], sch.sem["c"]

    def p0():
        def ld(e):
            out = []
            out.append(e.dma_start(out=ident, in_=ident_in))
            out.append(e.dma_start(out=mask, in_=mask_in))
            out.append(e.dma_start(out=bqk_sb, in_=bqk_in))
            out.append(e.dma_start(out=bv_sb, in_=bv_in))
            out.append(e.dma_start(out=bpr_sb, in_=bpr_in))
            out.append(e.dma_start(out=bo_sb, in_=bout_in))
            out.append(e.dma_start(out=bffn_sb, in_=bffn_in))
            out.append(e.dma_start(out=qnw_sb, in_=qnw_in))
            out.append(e.dma_start(out=knw_sb, in_=knw_in))
            out.append(e.dma_start(out=wbuf[2][:, 0:256], in_=dw_in))
            for c in range(8):
                out.append(e.dma_start(out=wqk[c], in_=wqk_in[c * 128:(c + 1) * 128, :]))
            return out
        sch.dma(ld, n=18)

        def consts(e):
            e.memset(ones1, 1.0)
            e.memset(ones128, 1.0)
            e.memset(c_ln2h, LN2H)
            e.memset(c_four, 4.0)
            return e.memset(c_0125, 0.125)
        sch.g(consts)

        def dload(e):
            return e.dma_start(out=d_w1, in_=dact_in)
        sch.dma(dload)
        def dtanh(e):
            e.activation(d_w2, d_w1, AF.Tanh, scale=c_four)
            return e.activation(d_w1, d_w2, AF.Exp, scale=c_ln2h)
        sch.s(dtanh, waits=sch.snap("d", "g"))
        def dmuls(e):
            last = None
            vals = TH_A + TH_B + TH_C + [0.5]
            tgts = thA + thB + thC + [half2d]
            for th, tgt in zip(vals[:-1], tgts[:-1]):
                e.raw.tensor_scalar(tgt, d_w1, float(th), None, op0=A.mult)
            return e.tensor_scalar(tgts[-1], d_w1, float(vals[-1]), None, op0=A.mult)
        sch.v(dmuls, waits=sch.snap("s"))

        for c in range(8):
            buf = wbuf[c % 2]
            def wl(e, c=c, buf=buf):
                return e.dma_start(out=buf, in_=wffn_in[c * 128:(c + 1) * 128, :])
            sch.dma(wl, waits=sch.snap("v", "s"))

            dwc = wbuf[2][:, c * 32 + 0:(c + 1) * 32]
            def wth(e, dwc=dwc):
                e.activation(w_t1, dwc, AF.Tanh, scale=c_four)
                return e.activation(w_twod, w_t1, AF.Exp, scale=c_ln2h)
            sch.s(wth, waits=sch.snap("d", "g"))

            def wstats(e, buf=buf):
                for i, th in enumerate(TH_A + TH_B + TH_C):
                    e.raw.tensor_scalar(w_th[i], w_twod, float(th), None, op0=A.mult)
                e.drain()
                e.tensor_reduce(q_am, buf.rearrange("p (b i) -> p b i", i=32),
                                mybir.AxisListType.X, A.max, apply_absolute_value=True)
                e.tensor_scalar(q_m, q_am, TINY, None, op0=A.max)
                e.memset(q_p, 2.0 ** -100)
                for k in (64, 32, 16, 8, 4, 2, 1):
                    e.tensor_scalar(q_t, q_p, float(2.0 ** k), None, op0=A.mult)
                    e.tensor_tensor(q_msk, q_m, q_t, op=A.is_ge)
                    e.tensor_tensor(q_tm, q_t, q_msk, op=A.mult)
                    e.tensor_tensor(q_p, q_p, q_tm, op=A.max)
                e.tensor_scalar(q_msk, q_am, 0.0, None, op0=A.is_gt)
                e.tensor_scalar(q_t, q_p, 0.25, 1.0, op0=A.mult, op1=A.subtract)
                e.tensor_tensor(q_tm, q_t, q_msk, op=A.mult)
                e.tensor_scalar(q_base, q_tm, 1.0, None, op0=A.add)
                e.reciprocal(q_rb, q_base)
                e.tensor_tensor(q_f, q_base, w_twod, op=A.mult)
                return e.tensor_scalar(q_f, q_f, 0.5, None, op0=A.mult)
            sch.v(wstats, waits=sch.snap("d", "s"))

            def wabs(e, buf=buf):
                e.activation(junk, buf, AF.Abs)
                return e.sign(xln2b[0], buf)
            sch.s(wabs, waits=sch.snap("d"))

            def wquant(e):
                qrB = p_sb[:, 0:D]
                qrC = p_sb[:, D:2 * D]
                bc = lambda t: t.rearrange("p (b o) -> p b o", o=1).to_broadcast([128, 32, 32])
                r3 = lambda t: t.rearrange("p (b i) -> p b i", i=32)
                e.tensor_tensor(r3(qaxs), r3(junk), bc(q_rb), op=A.mult)
                e.tensor_tensor(r3(qrA), r3(qaxs), bc(w_th[0]), op=A.is_ge)
                for i in (1, 2, 3):
                    e.tensor_tensor(r3(qrC), r3(qaxs), bc(w_th[i]), op=A.is_ge)
                    e.tensor_tensor(qrA, qrA, qrC, op=A.add)
                e.tensor_tensor(r3(qrB), r3(qaxs), bc(w_th[4]), op=A.is_ge)
                e.tensor_tensor(r3(qrC), r3(qaxs), bc(w_th[5]), op=A.is_ge)
                e.tensor_tensor(qrB, qrB, qrC, op=A.add)
                e.tensor_tensor(r3(qrC), r3(qaxs), bc(w_th[6]), op=A.is_ge)
                e.scalar_tensor_tensor(qrC, qrC, 2.0, qrB, op0=A.mult, op1=A.add)
                e.scalar_tensor_tensor(qrC, qrC, 2.0, qrA, op0=A.mult, op1=A.add)
                e.tensor_tensor(r3(qrB), r3(xln2b[0]), bc(q_f), op=A.mult)
                return e.tensor_tensor(x_sb2[0], qrC, qrB, op=A.mult)
            sch.v(wquant, waits=sch.snap("s"))

            def wtr(e, c=c):
                out = []
                for r in range(8):
                    out.append(e.transpose(ps_med[:, r * 128:(r + 1) * 128],
                                           x_sb2[0][:, r * 128:(r + 1) * 128], ident))
                return out[-1]
            sch.t(wtr, waits=sch.snap("v"))

            def wev(e, c=c):
                for r in range(7):
                    e.raw.tensor_copy(wqt[r][:, c * 128:(c + 1) * 128],
                                      ps_med[:, r * 128:(r + 1) * 128])
                return e.tensor_copy(wqt[7][:, c * 128:(c + 1) * 128],
                                     ps_med[:, 7 * 128:8 * 128])
            sch.v(wev, waits=sch.snap("t"))

    def emit_ln(x_ap, out_ap):
        def vsum(e):
            return e.reduce_sum(l_sum, x_ap, mybir.AxisListType.X)
        sch.v(vsum, waits=sch.snap("d", "t", "s", "g"))

        def ssq(e):
            return e.activation(junk, x_ap, AF.Square, accum_out=l_ssq)
        sch.s(ssq, waits=sch.snap("d", "t", "g", "v"))

        def stats(e):
            e.tensor_scalar(l_m, l_sum, 1.0 / D, None, op0=A.mult)
            e.tensor_scalar(l_t0, l_ssq, 1.0 / D, None, op0=A.mult)
            e.tensor_tensor(l_t1, l_m, l_m, op=A.mult)
            e.tensor_tensor(l_var, l_t0, l_t1, op=A.subtract)
            e.tensor_scalar(l_var, l_var, LN_EPS, None, op0=A.add)
            return e.reciprocal(l_t0, l_var)
        sch.v(stats, waits=sch.snap("s"))

        def sq(e):
            return e.activation(l_r0, l_t0, AF.Sqrt)
        sch.s(sq, waits=sch.snap("v"))

        def fin(e):
            e.tensor_tensor(l_t1, l_r0, l_r0, op=A.mult)
            e.tensor_tensor(l_t1, l_t1, l_var, op=A.mult)
            e.tensor_scalar(l_t1, l_t1, -0.5, 1.5, op0=A.mult, op1=A.add)
            e.tensor_tensor(l_rstd, l_r0, l_t1, op=A.mult)
            return e.tensor_scalar(out_ap, x_ap, l_m, l_rstd,
                                   op0=A.subtract, op1=A.mult)
        sch.v(fin, waits=sch.snap("s", "o"))

    def emit_transpose8(src_ap, ps, dst_ap):
        def tr(e):
            last = None
            for r in range(8):
                last = e.transpose(ps[:, r * 128:(r + 1) * 128],
                                   src_ap[:, r * 128:(r + 1) * 128], ident)
            return last
        sch.t(tr, waits=sch.snap("v", "s", "g", "d"))

        def ev(e):
            return e.tensor_copy(dst_ap, ps)
        sch.v(ev, waits=sch.snap("t"))

    def emit_quant_act(x_ap, out_ap, qrB, qrC):
        r3 = lambda t: t.rearrange("p (b i) -> p b i", i=32)
        bc = lambda t: t.rearrange("p (b o) -> p b o", o=1).to_broadcast([128, 32, 32])

        def vstat(e):
            e.tensor_reduce(q_am, r3(x_ap), mybir.AxisListType.X, A.max,
                            apply_absolute_value=True)
            e.tensor_scalar(q_m, q_am, TINY, None, op0=A.max)
            e.memset(q_p, 2.0 ** -100)
            for k in (64, 32, 16, 8, 4, 2, 1):
                e.tensor_scalar(q_t, q_p, float(2.0 ** k), None, op0=A.mult)
                e.tensor_tensor(q_msk, q_m, q_t, op=A.is_ge)
                e.tensor_tensor(q_tm, q_t, q_msk, op=A.mult)
                e.tensor_tensor(q_p, q_p, q_tm, op=A.max)
            e.tensor_scalar(q_msk, q_am, 0.0, None, op0=A.is_gt)
            e.tensor_scalar(q_t, q_p, 0.25, 1.0, op0=A.mult, op1=A.subtract)
            e.tensor_tensor(q_tm, q_t, q_msk, op=A.mult)
            e.tensor_scalar(q_base, q_tm, 1.0, None, op0=A.add)
            e.reciprocal(q_rb, q_base)
            return e.tensor_tensor(q_f, q_base, half2d.to_broadcast([128, 32]), op=A.mult)
        sch.v(vstat, waits=sch.snap("s", "t", "g", "d"))

        def sabs2(e):
            e.activation(junk, x_ap, AF.Abs)
            return e.sign(xpb, x_ap)
        sch.s(sabs2, waits=sch.snap("v", "t", "g", "d"))

        def vaxs(e):
            return e.tensor_tensor(r3(qaxs), r3(junk), bc(q_rb), op=A.mult)
        sch.v(vaxs, waits=sch.snap("s"))

        def vA(e):
            e.tensor_scalar(qrA, qaxs, thA[0], None, op0=A.is_ge)
            for i in (1, 2, 3):
                e.scalar_tensor_tensor(qrA, qaxs, thA[i], qrA,
                                       op0=A.is_ge, op1=A.add)
            e.tensor_scalar(qrB, qaxs, thB[0], None, op0=A.is_ge)
            e.scalar_tensor_tensor(qrB, qaxs, thB[1], qrB, op0=A.is_ge, op1=A.add)
            e.tensor_scalar(qrC, qaxs, thC[0], None, op0=A.is_ge)
            return e.scalar_tensor_tensor(qrC, qrC, 2.0, qrB, op0=A.mult, op1=A.add)
        sch.v(vA, waits=sch.snap("t"))

        def vfin(e):
            e.scalar_tensor_tensor(qrC, qrC, 2.0, qrA, op0=A.mult, op1=A.add)
            e.tensor_tensor(r3(qrB), r3(xpb), bc(q_f), op=A.mult)
            return e.tensor_tensor(out_ap, qrC, qrB, op=A.mult)
        sch.v(vfin, waits=sch.snap("s"))

    def stream_w(src_ap, dc, buf, extra_waits=()):
        def wl(e):
            return e.dma_start(out=buf, in_=src_ap[dc * 128:(dc + 1) * 128, :])
        sch.dma(wl, waits=list(extra_waits), sem="w")

    def p1_front(i):
        p = i % 2
        row0 = i * 128

        def ldx(e):
            return e.dma_start(out=x_sb2[p], in_=x_in[row0:row0 + 128, :])
        sch.dma(ldx, waits=sch.snap("v", "s", "t", "g", "o"))

        emit_ln(x_sb2[p], xln2b[p])
        emit_transpose8(xln2b[p], ps_aux, xlnT2[p])

    def p1_back(i):
        p = i % 2
        row0 = i * 128
        x_sb, xln, xlnT = x_sb2[p], xln2b[p], xlnT2[p]
        u_sb = x_sb
        qrB = p_sb[:, 0:D]
        qrC = p_sb[:, D:2 * D]

        def qkmm(e):
            last = None
            for oc in range(16):
                for dc in range(8):
                    last = e.matmul(ps_big[:, oc * 128:(oc + 1) * 128],
                                    wqk[dc][:, oc * 128:(oc + 1) * 128],
                                    xlnT[:, dc * 128:(dc + 1) * 128],
                                    start=(dc == 0), stop=(dc == 7))
            return last
        sch.t(qkmm, waits=sch.snap("v", "d"))

        for dc in range(8):
            stream_w(wv_in, dc, wbuf[dc % 2], extra_waits=sch.snap("t"))
            def vmm(e, dc=dc):
                last = None
                for ob in range(2):
                    last = e.matmul(ps_med[:, ob * 512:(ob + 1) * 512],
                                    xlnT[:, dc * 128:(dc + 1) * 128],
                                    wbuf[dc % 2][:, ob * 512:(ob + 1) * 512],
                                    start=(dc == 0), stop=False)
                return last
            sch.t(vmm, waits=sch.snap("w"))
        def vbias(e):
            e.matmul(ps_med[:, 0:512], ones1, bv_sb[:, 0:512], start=False, stop=True)
            return e.matmul(ps_med[:, 512:1024], ones1, bv_sb[:, 512:1024],
                            start=False, stop=True)
        sch.t(vbias)

        def evqk(e):
            for oc in range(16):
                e.raw.tensor_scalar(qkT[:, oc * 128:(oc + 1) * 128],
                                    ps_big[:, oc * 128:(oc + 1) * 128],
                                    bqk_sb[:, oc:oc + 1], None, op0=A.add)
            return e.tensor_copy(v_sb, ps_med)
        sch.v(evqk, waits=sch.snap("t"))

        def rebase(e):
            out = []
            for oc in range(16):
                out.append(e.dma_start(out=qkOdd[0:64, oc * 128:(oc + 1) * 128],
                                       in_=qkT[64:128, oc * 128:(oc + 1) * 128]))
            return out
        sch.dma(rebase, waits=sch.snap("v"), n=16)

        def qh(h):
            oc = h // 2
            src = qkT if h % 2 == 0 else qkOdd
            return src[0:64, oc * 128:(oc + 1) * 128]

        def kh(h):
            oc = 8 + h // 2
            src = qkT if h % 2 == 0 else qkOdd
            return src[0:64, oc * 128:(oc + 1) * 128]

        def smm(e):
            last = None
            for h in range(16):
                last = e.matmul(ps_big[:, h * 128:(h + 1) * 128], qh(h), kh(h),
                                start=True, stop=True)
            return last
        sch.t(smm, waits=sch.snap("d", "v"))

        def sm1(e):
            e.tensor_reduce(a_hm, ps_big.rearrange("p (h j) -> p h j", j=128),
                            mybir.AxisListType.X, A.max)
            e.reduce_max(a_cm, a_hm, mybir.AxisListType.X)
            return e.tensor_scalar(a_negc, a_cm, -0.125, None, op0=A.mult)
        sch.v(sm1, waits=sch.snap("t"))

        def sexp(e):
            return e.activation(p_sb, ps_big, AF.Exp, bias=a_negc, scale=c_0125)
        sch.s(sexp, waits=sch.snap("v"))

        def smask(e):
            e.tensor_tensor(p_sb.rearrange("p (h j) -> p h j", j=128),
                            p_sb.rearrange("p (h j) -> p h j", j=128),
                            mask.rearrange("p (o j) -> p o j", o=1)
                                .to_broadcast([128, 16, 128]),
                            op=A.mult)
            e.tensor_reduce(a_rs, p_sb.rearrange("p (h j) -> p h j", j=128),
                            mybir.AxisListType.X, A.add)
            return e.reciprocal(a_rr, a_rs)
        sch.v(smask, waits=sch.snap("s"))

        def ptr(e):
            last = None
            for h in range(16):
                last = e.transpose(ps_big[:, h * 128:(h + 1) * 128],
                                   p_sb[:, h * 128:(h + 1) * 128], ident)
            return last
        sch.t(ptr, waits=sch.snap("v"))

        def ptev(e):
            return e.tensor_copy(qkT, ps_big)
        sch.v(ptev, waits=sch.snap("t"))

        def avmm(e):
            last = None
            for h in range(16):
                last = e.matmul(ps_med[:, h * 64:(h + 1) * 64],
                                qkT[:, h * 128:(h + 1) * 128],
                                v_sb[:, h * 64:(h + 1) * 64],
                                start=True, stop=True)
            return last
        sch.t(avmm, waits=sch.snap("v"))

        def avev(e):
            return e.tensor_tensor(
                av_sb.rearrange("p (h j) -> p h j", j=64),
                ps_med.rearrange("p (h j) -> p h j", j=64),
                a_rr.rearrange("p (h o) -> p h o", o=1).to_broadcast([128, 16, 64]),
                op=A.mult)
        sch.v(avev, waits=sch.snap("t"))

        emit_transpose8(av_sb, ps_aux, avT)

        for dc in range(8):
            stream_w(wpr_in, dc, wbuf[2 if dc % 2 == 0 else 0], extra_waits=sch.snap("t"))
            def pmm(e, dc=dc):
                last = None
                for ob in range(2):
                    last = e.matmul(ps_med[:, ob * 512:(ob + 1) * 512],
                                    avT[:, dc * 128:(dc + 1) * 128],
                                    wbuf[2 if dc % 2 == 0 else 0][:, ob * 512:(ob + 1) * 512],
                                    start=(dc == 0), stop=False)
                return last
            sch.t(pmm, waits=sch.snap("w"))
        def pbias(e):
            e.matmul(ps_med[:, 0:512], ones1, bpr_sb[:, 0:512], start=False, stop=True)
            return e.matmul(ps_med[:, 512:1024], ones1, bpr_sb[:, 512:1024],
                            start=False, stop=True)
        sch.t(pbias)

        def resid1(e):
            return e.tensor_tensor(x1_sb, ps_med, x_sb, op=A.add)
        sch.v(resid1, waits=sch.snap("t"))

        emit_ln(x1_sb, xln)
        emit_quant_act(xln, u_sb, qrB, qrC)
        emit_transpose8(u_sb, ps_aux, uT)

        def fmm(e):
            last = None
            for ob in range(2):
                for dc in range(8):
                    last = e.matmul(ps_med[:, ob * 512:(ob + 1) * 512],
                                    uT[:, dc * 128:(dc + 1) * 128],
                                    wqt[dc][:, ob * 512:(ob + 1) * 512],
                                    start=(dc == 0), stop=False)
                last = e.matmul(ps_med[:, ob * 512:(ob + 1) * 512], ones1,
                                bffn_sb[:, ob * 512:(ob + 1) * 512],
                                start=False, stop=True)
            return last
        sch.t(fmm, waits=sch.snap("v"))

        def resid2(e):
            return e.tensor_tensor(xln, ps_med, x1_sb, op=A.add)
        sch.v(resid2, waits=sch.snap("t"))

        def stx2(e):
            return e.dma_start(out=x2d[row0:row0 + 128, :], in_=xln)
        sch.dma(stx2, waits=sch.snap("v"), sem="o")

        if upto == "p1":
            def sty(e):
                return e.dma_start(out=y_out[row0:row0 + 128, :], in_=xln)
            sch.dma(sty, waits=sch.snap("v"), sem="o")

    x2d_bm = x2d.rearrange("(s b) d -> b s d", b=B)

    def p2_front(j):
        p = j % 2
        b, s0 = j // 8, (j % 8) * 128

        def ldx(e):
            return e.dma_start(out=x_sb2[p], in_=x2d_bm[b, s0:s0 + 128, :])
        sch.dma(ldx, waits=sch.snap("v", "s", "t", "g", "o"))

        emit_ln(x_sb2[p], xln2b[p])
        emit_transpose8(xln2b[p], ps_aux, xlnT2[p])

    def p2_back(j):
        p = j % 2
        b = j // 8
        xlnT = xlnT2[p]

        for dc in range(8):
            for part in range(3):
                stream_w(wqb_in.rearrange("d (pp o) -> pp d o", pp=3)[part], dc,
                         wbuf[(dc * 3 + part) % 3],
                         extra_waits=sch.snap("t"))
            def qmm(e, dc=dc):
                tgts = [ps_med, ps_aux, ps_big[:, 0:1024]]
                last = None
                for part in range(3):
                    for ob in range(2):
                        last = e.matmul(tgts[part][:, ob * 512:(ob + 1) * 512],
                                        xlnT[:, dc * 128:(dc + 1) * 128],
                                        wbuf[(dc * 3 + part) % 3][:, ob * 512:(ob + 1) * 512],
                                        start=(dc == 0), stop=(dc == 7))
                return last
            sch.t(qmm, waits=sch.snap("w", "v"))

        def rms(ps, wrep, dst):
            def s1(e):
                return e.activation(junk, ps, AF.Square, accum_out=r_ssq)
            sch.s(s1, waits=sch.snap("t", "v"))
            def v1(e):
                e.tensor_scalar(r_t0, r_ssq, 1.0 / D, RMS_EPS, op0=A.mult, op1=A.add)
                return e.reciprocal(r_t1, r_t0)
            sch.v(v1, waits=sch.snap("s"))
            def s2(e):
                return e.activation(r_r0, r_t1, AF.Sqrt)
            sch.s(s2, waits=sch.snap("v"))
            def v2(e):
                e.tensor_tensor(r_rq, r_r0, r_r0, op=A.mult)
                e.tensor_tensor(r_rq, r_rq, r_t0, op=A.mult)
                e.tensor_scalar(r_rq, r_rq, -0.5, 1.5, op0=A.mult, op1=A.add)
                e.tensor_tensor(r_rq, r_r0, r_rq, op=A.mult)
                e.tensor_scalar(junk, ps, r_rq, None, op0=A.mult)
                e.tensor_tensor(junk, junk, wrep, op=A.mult)
                return e.tensor_scalar(dst, junk, 0.0, None, op0=A.max)
            sch.v(v2, waits=sch.snap("s"))

        qn_ap = qkT[:, 0:1024]
        kn_ap = qkT[:, 1024:2048]
        rms(ps_med, qnw_sb, qn_ap)
        rms(ps_aux, knw_sb, kn_ap)
        def vev(e):
            return e.tensor_copy(v_sb, ps_big[:, 0:1024])
        sch.v(vev, waits=sch.snap("t"))

        def kvmm(e):
            last = None
            for h in range(16):
                last = e.matmul(ps_big[0:64, 1024 + h * 64:1024 + (h + 1) * 64],
                                kn_ap[:, h * 64:(h + 1) * 64],
                                v_sb[:, h * 64:(h + 1) * 64],
                                start=True, stop=True)
            for h in range(16):
                last = e.matmul(ps_aux[0:64, h:h + 1],
                                kn_ap[:, h * 64:(h + 1) * 64], ones128,
                                start=True, stop=True)
            return last
        sch.t(kvmm, waits=sch.snap("v"))

        def kvacc(e):
            e.raw.tensor_tensor(kvacc_ap(b), ps_big[0:64, 1024:2048],
                                kvacc_ap(b), op=A.add)
            return e.tensor_tensor(ksacc_ap(b), ps_aux[0:64, 0:16],
                                   ksacc_ap(b), op=A.add)
        sch.v(kvacc, waits=sch.snap("t"))

        def stqn(e):
            return e.dma_start(out=qnd[j * 128:(j + 1) * 128, :], in_=qn_ap)
        sch.dma(stqn, waits=sch.snap("v"), sem="o")

    def p3():
        def st(e):
            out = []
            for bb in range(B):
                out.append(e.dma_start(out=cc_in.ap()[:, bb * 1024:(bb + 1) * 1024],
                                       in_=kvacc_ap(bb)))
            out.append(e.dma_start(out=cc_in.ap()[:, 4096:4160],
                                   in_=wqk[2][0:64, 0:64]))
            return out
        sch.dma(st, waits=sch.snap("v"), n=5)

        def cc(e):
            return e.collective_compute(
                "AllReduce", A.add, replica_groups=[list(range(NCORES))],
                ins=[cc_in.ap().opt()], outs=[cc_out.ap().opt()])
        sch.emit("gpsimd", cc, waits=sch.snap("d"), inc=scc, n=1, amt=1)

        def ld(e):
            o1 = e.dma_start(out=wqk[3][0:64, :], in_=cc_out.ap()[:, 0:2048])
            o2 = e.dma_start(out=wqk[4][0:64, :], in_=cc_out.ap()[:, 2048:4096])
            o3 = e.dma_start(out=wqk[5][0:64, 0:64], in_=cc_out.ap()[:, 4096:4160])
            return [o1, o2, o3]
        sch.dma(ld, waits=sch.snap("c"), n=3)

    def p4_front(j):
        p = j % 2
        b, s0 = j // 8, (j % 8) * 128
        qn_ap = p_sb[:, p * 1024:(p + 1) * 1024]

        def ldq(e):
            o1 = e.dma_start(out=qn_ap, in_=qnd[j * 128:(j + 1) * 128, :])
            o2 = e.dma_start(out=x_sb2[p], in_=x2d_bm[b, s0:s0 + 128, :])
            return [o1, o2]
        sch.dma(ldq, waits=sch.snap("v", "s", "t", "g", "o"), n=2)

        emit_transpose8(qn_ap, ps_aux, xlnT2[p])

        def rebase(e):
            out = []
            for dc in range(8):
                out.append(e.dma_start(
                    out=qkOdd[0:64, p * 1024 + dc * 128:p * 1024 + (dc + 1) * 128],
                    in_=xlnT2[p][64:128, dc * 128:(dc + 1) * 128]))
            return out
        sch.dma(rebase, waits=sch.snap("v"), n=8)

    def p4_back(j):
        p = j % 2
        b = j // 8
        x_sb, xln, xlnT = x_sb2[p], xln2b[p], xlnT2[p]
        u_sb = x_sb
        qrB = qkT[:, 0:D]
        qrC = qkT[:, D:2 * D]

        def qh(h):
            dc = h // 2
            if h % 2 == 0:
                return xlnT[0:64, dc * 128:(dc + 1) * 128]
            return qkOdd[0:64, p * 1024 + dc * 128:p * 1024 + (dc + 1) * 128]

        def attmm(e):
            last = None
            for h in range(16):
                last = e.matmul(ps_med[:, h * 64:(h + 1) * 64], qh(h),
                                kvg_kv(b, h), start=True, stop=True)
            for h in range(16):
                last = e.matmul(ps_aux[:, h:h + 1], qh(h),
                                kvg_ks(b, h), start=True, stop=True)
            return last
        sch.t(attmm, waits=sch.snap("d", "v"))

        def nrm(e):
            e.tensor_scalar(nrm_sb, ps_aux[:, 0:16], DEN_EPS, None, op0=A.add)
            e.reciprocal(nrr_sb, nrm_sb)
            return e.tensor_tensor(
                av_sb.rearrange("p (h j) -> p h j", j=64),
                ps_med.rearrange("p (h j) -> p h j", j=64),
                nrr_sb.rearrange("p (h o) -> p h o", o=1).to_broadcast([128, 16, 64]),
                op=A.mult)
        sch.v(nrm, waits=sch.snap("t"))

        emit_transpose8(av_sb, ps_aux, avT)

        for dc in range(8):
            stream_w(wout_in, dc, wbuf[dc % 2], extra_waits=sch.snap("t"))
            def omm(e, dc=dc):
                last = None
                for ob in range(2):
                    last = e.matmul(ps_med[:, ob * 512:(ob + 1) * 512],
                                    avT[:, dc * 128:(dc + 1) * 128],
                                    wbuf[dc % 2][:, ob * 512:(ob + 1) * 512],
                                    start=(dc == 0), stop=False)
                return last
            sch.t(omm, waits=sch.snap("w"))
        def obias(e):
            e.matmul(ps_med[:, 0:512], ones1, bo_sb[:, 0:512], start=False, stop=True)
            return e.matmul(ps_med[:, 512:1024], ones1, bo_sb[:, 512:1024],
                            start=False, stop=True)
        sch.t(obias)

        def resid3(e):
            return e.tensor_tensor(x1_sb, ps_med, x_sb, op=A.add)
        sch.v(resid3, waits=sch.snap("t"))

        emit_ln(x1_sb, xln)
        emit_quant_act(xln, u_sb, qrB, qrC)
        emit_transpose8(u_sb, ps_aux, uT)

        def fmm(e):
            last = None
            for ob in range(2):
                for dc in range(8):
                    last = e.matmul(ps_med[:, ob * 512:(ob + 1) * 512],
                                    uT[:, dc * 128:(dc + 1) * 128],
                                    wqt[dc][:, ob * 512:(ob + 1) * 512],
                                    start=(dc == 0), stop=False)
                last = e.matmul(ps_med[:, ob * 512:(ob + 1) * 512], ones1,
                                bffn_sb[:, ob * 512:(ob + 1) * 512],
                                start=False, stop=True)
            return last
        sch.t(fmm, waits=sch.snap("v"))

        def resid4(e):
            return e.tensor_tensor(xln, ps_med, x1_sb, op=A.add)
        sch.v(resid4, waits=sch.snap("t"))

        def sty(e):
            return e.dma_start(out=y_out[j * 128:(j + 1) * 128, :], in_=xln)
        sch.dma(sty, waits=sch.snap("v"), sem="o")

    p0()
    p1_front(0)
    for i in range(nt):
        if i + 1 < nt:
            p1_front(i + 1)
        p1_back(i)

    if upto != "p1":
        def zacc(e):
            for bb in range(B):
                e.memset(kvacc_ap(bb), 0.0)
                e.memset(ksacc_ap(bb), 0.0)
            return e.memset(wqk[2][0:64, 0:64], 0.0)
        sch.g(zacc, waits=sch.snap("v", "t"))
        p2_front(0)
        for j in range(nt):
            if j + 1 < nt:
                p2_front(j + 1)
            p2_back(j)
        if upto not in ("p2", "p3"):
            p3()
            p4_front(0)
            for j in range(nt):
                if j + 1 < nt:
                    p4_front(j + 1)
                p4_back(j)
        elif upto == "p3":
            p3()

    o_final = so.n

    def fin(e):
        e.wait_ge(so.h, o_final)
        return None
    sch.emit("sync", fin, inc=None)

    with nc.Block() as block:
        @block.sync
        def _(e):
            for fn in sch.q["sync"]:
                fn(e)

        @block.vector
        def _(e):
            e = _Drainer(e)
            for fn in sch.q["vector"]:
                fn(e)

        @block.scalar
        def _(e):
            e = _Drainer(e)
            for fn in sch.q["scalar"]:
                fn(e)

        @block.tensor
        def _(e):
            for fn in sch.q["tensor"]:
                fn(e)

        @block.gpsimd
        def _(e):
            e = _Drainer(e)
            for fn in sch.q["gpsimd"]:
                fn(e)

    stack.close()
    return nc


def stage_inputs(inputs, core):
    x = inputs["L0_attn_a_x"]
    xs = x[:, core * S_CORE:(core + 1) * S_CORE, :]
    x_core = np.ascontiguousarray(np.transpose(xs, (1, 0, 2)).reshape(T_CORE, D))

    wqkv = inputs["L0_attn_a_Wqkv"]
    bqkv = inputs["L0_attn_a_bqkv"]
    dw = inputs["L0_ffn_delta_w"]

    return {
        "x": x_core,
        "wqk_t": np.ascontiguousarray(wqkv[0:2048, :].T),
        "bqk_pp": np.ascontiguousarray(bqkv[0:2048].reshape(16, 128).T),
        "wv_t": np.ascontiguousarray(wqkv[2048:3072, :].T),
        "bv_row": np.ascontiguousarray(bqkv[2048:3072][None, :]),
        "wproj_t": np.ascontiguousarray(inputs["L0_attn_a_Wproj"].T),
        "bproj_row": np.ascontiguousarray(inputs["L0_attn_a_bproj"][None, :]),
        "wffn": np.ascontiguousarray(inputs["L0_ffn_weight"]),
        "dw": np.ascontiguousarray(
            np.transpose(dw.reshape(8, 128, 32), (1, 0, 2)).reshape(128, 256)),
        "bffn_row": np.ascontiguousarray(inputs["L0_ffn_bias"][None, :]),
        "dact": np.ascontiguousarray(np.broadcast_to(
            inputs["L0_ffn_delta_act"].reshape(1, 1), (128, 1))),
        "wqkv_b": np.ascontiguousarray(inputs["L0_attn_b_W_qkv"]),
        "qnw_rep": np.ascontiguousarray(
            np.broadcast_to(inputs["L0_attn_b_qn_w"][None, :], (128, D))),
        "knw_rep": np.ascontiguousarray(
            np.broadcast_to(inputs["L0_attn_b_kn_w"][None, :], (128, D))),
        "wout": np.ascontiguousarray(inputs["L0_attn_b_W_out"]),
        "bout_row": np.ascontiguousarray(inputs["L0_attn_b_b_out"][None, :]),
        "ident": np.eye(128, dtype=np.float32),
        "mask": np.kron(np.eye(32, dtype=np.float32),
                        np.ones((4, 4), dtype=np.float32)),
    }


def kernel(**inputs):
    inputs = {k: np.asarray(v, dtype=np.float32) for k, v in inputs.items()}
    nc = build_program("all")
    in_maps = [stage_inputs(inputs, c) for c in range(NCORES)]
    res = run_bass_kernel_spmd(nc, in_maps, core_ids=list(range(NCORES)))
    out = np.empty((B, S, D), dtype=np.float32)
    for c in range(NCORES):
        yc = np.asarray(res.results[c]["y"], dtype=np.float32).reshape(B, S_CORE, D)
        out[:, c * S_CORE:(c + 1) * S_CORE, :] = yc
    return out


# version 08
# speedup vs baseline: 1.0608x; 1.0608x over previous
import math
import contextlib
import numpy as np
import concourse.bass as bass
import bass_rust
import concourse.mybir as mybir
from concourse.bass_utils import run_bass_kernel_spmd

F32 = mybir.dt.float32
A = mybir.AluOpType
AF = mybir.ActivationFunctionType

NCORES = 8
B, S, D = 4, 8192, 1024
S_CORE = S // NCORES
T_CORE = B * S_CORE
NT = T_CORE // 128
H, DH = 16, 64
LN_EPS = 1e-5
RMS_EPS = 1e-6
DEN_EPS = 1e-6
TINY = 1e-30
TH_A = [0.25, 0.75, 1.25, 1.75]
TH_B = [2.5, 3.5]
TH_C = [5.0]
LN2H = math.log(2.0) / 2.0

ENGINES = ("sync", "vector", "scalar", "tensor", "gpsimd")


class _Drainer:

    _PASS = {"wait_ge", "drain"}

    def __init__(self, e):
        self._e = e
        self.raw = e

    def __getattr__(self, name):
        f = getattr(self._e, name)
        if name in self._PASS or not callable(f):
            return f
        def wrapper(*a, **k):
            r = f(*a, **k)
            self._e.drain()
            return r
        return wrapper


class Sem:
    def __init__(self, handle):
        self.h = handle
        self.n = 0


class Sched:

    def __init__(self, nc, stack):
        self.nc = nc
        self.q = {e: [] for e in ENGINES}
        self.sem = {}
        for name in ("d", "v", "s", "t", "g", "o", "c", "w", "fv", "fs", "ft"):
            self.sem[name] = Sem(stack.enter_context(nc.semaphore("sem_" + name)))

    def emit(self, eng, builder, waits=(), inc=None, n=1, amt=1):
        frozen = tuple((s, c) for (s, c) in waits if c > 0)

        def fn(e):
            for s, c in frozen:
                e.wait_ge(s.h, c)
            out = builder(e)
            if inc is not None:
                instrs = out if isinstance(out, (list, tuple)) else [out]
                assert len(instrs) == n, f"expected {n} instrs, got {len(instrs)}"
                for ins in instrs:
                    ins.then_inc(inc.h, amt)

        self.q[eng].append(fn)
        if inc is not None:
            inc.n += n * amt

    def v(self, builder, waits=(), sem="v"):
        self.emit("vector", builder, waits, self.sem[sem])

    def s(self, builder, waits=(), sem="s"):
        self.emit("scalar", builder, waits, self.sem[sem])

    def t(self, builder, waits=(), sem="t"):
        self.emit("tensor", builder, waits, self.sem[sem])

    def tok(self, name):
        return (self.sem[name], self.sem[name].n)

    def g(self, builder, waits=()):
        self.emit("gpsimd", builder, waits, self.sem["g"])

    def dma(self, builder, waits=(), n=1, sem="d"):
        self.emit("sync", builder, waits, self.sem[sem], n=n, amt=16)

    def gdma(self, builder, waits=(), n=1, sem="g"):
        self.emit("gpsimd", builder, waits, self.sem[sem], n=n, amt=16)

    def snap(self, *names):
        return [(self.sem[k], self.sem[k].n) for k in names]


def build_program(upto="all", nt=NT, tap=None):
    nc = bass.Bass("TRN2", target_bir_lowering=False, debug=False,
                   detect_race_conditions=False, dynamic_dma_scratch_size=512)

    def din(name, shape):
        return nc.dram_tensor(name, shape, F32, kind="ExternalInput").ap()

    x_in = din("x", [T_CORE, D])
    wqk_in = din("wqk_t", [D, 2 * D])
    bqk_in = din("bqk_pp", [128, 16])
    wv_in = din("wv_t", [D, D])
    bv_in = din("bv_row", [1, D])
    wpr_in = din("wproj_t", [D, D])
    bpr_in = din("bproj_row", [1, D])
    wffn_in = din("wffn", [D, D])
    dw_in = din("dw", [128, 8 * 32])
    bffn_in = din("bffn_row", [1, D])
    dact_in = din("dact", [128, 1])
    wqb_in = din("wqkv_b", [D, 3 * D])
    qnw_in = din("qnw_rep", [128, D])
    knw_in = din("knw_rep", [128, D])
    wout_in = din("wout", [D, D])
    bout_in = din("bout_row", [1, D])
    ident_in = din("ident", [128, 128])
    mask_in = din("mask", [128, 128])

    y_out = nc.dram_tensor("y", [T_CORE, D], F32, kind="ExternalOutput").ap()

    x2d = nc.dram_tensor("x2d", [T_CORE, D], F32).ap()
    qnd = nc.dram_tensor("qnd", [T_CORE, D], F32).ap()
    cc_in = nc.dram_tensor("cc_in", [64, 4160], F32)
    cc_out = nc.dram_tensor("cc_out", [64, 4160], F32)

    def sbt(name, shape, dt=F32):
        return nc.alloc_sbuf_tensor(name, shape, dt).ap()

    wqk = [sbt(f"wqk{c}", [128, 2 * D]) for c in range(8)]
    wqt = [sbt(f"wqt{c}", [128, D]) for c in range(8)]
    wbuf = [sbt(f"wbuf{i}", [128, D]) for i in range(5)]

    x_sb2 = [sbt(f"x_sb{p}", [128, D]) for p in range(2)]
    xln2b = [sbt(f"xln{p}", [128, D]) for p in range(2)]
    xlnT2 = [sbt(f"xlnT{p}", [128, D]) for p in range(2)]
    qkT = sbt("qkT", [128, 2 * D])
    qkOdd = sbt("qkOdd", [64, 2 * D])
    v_sb = sbt("v_sb", [128, D])
    p_sb = sbt("p_sb", [128, 2 * D])
    av_sb = sbt("av_sb", [128, D])
    avT = sbt("avT", [128, D])
    x1_sb = sbt("x1_sb", [128, D])
    uT = sbt("uT", [128, D])
    junk = sbt("junk", [128, D])
    xpb = uT
    qaxs = av_sb
    qrA = avT

    ident = sbt("ident_sb", [128, 128])
    mask = sbt("mask_sb", [128, 128])
    bqk_sb = sbt("bqk_sb", [128, 16])
    brows = sbt("brows", [1, 4 * D])
    bv_sb = brows[:, 0:D]
    bpr_sb = brows[:, D:2 * D]
    bo_sb = brows[:, 2 * D:3 * D]
    bffn_sb = brows[:, 3 * D:4 * D]
    qnw_sb = sbt("qnw_sb", [128, D])
    knw_sb = sbt("knw_sb", [128, D])
    ones1 = sbt("ones1", [1, 128])
    ones128 = sbt("ones128", [128, 1])

    q_am = sbt("q_am", [128, 32])
    q_m = sbt("q_m", [128, 32])
    q_p = sbt("q_p", [128, 32])
    q_t = sbt("q_t", [128, 32])
    q_msk = sbt("q_msk", [128, 32])
    q_tm = sbt("q_tm", [128, 32])
    q_base = sbt("q_base", [128, 32])
    q_rb = sbt("q_rb", [128, 32])
    q_f = sbt("q_f", [128, 32])
    w_t1 = wbuf[4][:, 512:544]
    w_twod = wbuf[4][:, 544:576]
    w_th = [wbuf[4][:, 576 + 32 * i:608 + 32 * i] for i in range(7)]

    junk2 = sbt("junk2", [128, D])
    f_sum = sbt("f_sum", [128, 1])
    f_ssq = sbt("f_ssq", [128, 1])
    f_m = sbt("f_m", [128, 1])
    f_t0 = sbt("f_t0", [128, 1])
    f_t1 = sbt("f_t1", [128, 1])
    f_var = sbt("f_var", [128, 1])
    f_r0 = sbt("f_r0", [128, 1])
    f_rstd = sbt("f_rstd", [128, 1])
    l_sum = sbt("l_sum", [128, 1])
    l_ssq = sbt("l_ssq", [128, 1])
    l_m = sbt("l_m", [128, 1])
    l_t0 = sbt("l_t0", [128, 1])
    l_t1 = sbt("l_t1", [128, 1])
    l_var = sbt("l_var", [128, 1])
    l_r0 = sbt("l_r0", [128, 1])
    l_rstd = sbt("l_rstd", [128, 1])

    a_hm = sbt("a_hm", [128, 16])
    a_cm = sbt("a_cm", [128, 1])
    a_cmax = sbt("a_cmax", [128, 1])
    a_negc = sbt("a_negc", [128, 1])
    a_rs = sbt("a_rs", [128, 16])
    a_rr = sbt("a_rr", [128, 16])

    thA = [sbt(f"thA{i}", [128, 1]) for i in range(4)]
    thB = [sbt(f"thB{i}", [128, 1]) for i in range(2)]
    thC = [sbt(f"thC{i}", [128, 1]) for i in range(1)]
    half2d = sbt("half2d", [128, 1])
    d_w1 = sbt("d_w1", [128, 1])
    d_w2 = sbt("d_w2", [128, 1])
    c_ln2h = sbt("c_ln2h", [128, 1])
    c_four = sbt("c_four", [128, 1])
    c_0125 = sbt("c_0125", [128, 1])

    r_ssq = sbt("r_ssq", [128, 1])
    r_r0 = sbt("r_r0", [128, 1])
    r_t0 = sbt("r_t0", [128, 1])
    r_t1 = sbt("r_t1", [128, 1])
    r_rq = sbt("r_rq", [128, 1])

    def kvacc_ap(b):
        return wqk[b // 2][0:64, (b % 2) * 1024:(b % 2 + 1) * 1024]

    def ksacc_ap(b):
        return wqk[2][0:64, b * 16:(b + 1) * 16]

    def kvg_kv(b, h):
        col0 = b * 1024 + h * 64
        return wqk[3 + col0 // 2048][0:64, col0 % 2048:col0 % 2048 + 64]

    def kvg_ks(b, h):
        return wqk[5][0:64, b * 16 + h:b * 16 + h + 1]

    nrm_sb = sbt("nrm_sb", [128, 16])
    nrr_sb = sbt("nrr_sb", [128, 16])

    ps_big = nc.alloc_psum_tensor("ps_big", [128, 2 * D], F32).ap()
    ps_med = nc.alloc_psum_tensor("ps_med", [128, D], F32).ap()
    ps_aux = nc.alloc_psum_tensor("ps_aux", [128, D], F32).ap()

    stack = contextlib.ExitStack()
    sch = Sched(nc, stack)
    sd, sv, ssem, stn, sg = (sch.sem[k] for k in "dvstg")
    so, scc = sch.sem["o"], sch.sem["c"]

    def p0():
        def ld(e):
            out = []
            out.append(e.dma_start(out=ident, in_=ident_in))
            out.append(e.dma_start(out=mask, in_=mask_in))
            out.append(e.dma_start(out=bqk_sb, in_=bqk_in))
            out.append(e.dma_start(out=bv_sb, in_=bv_in))
            out.append(e.dma_start(out=bpr_sb, in_=bpr_in))
            out.append(e.dma_start(out=bo_sb, in_=bout_in))
            out.append(e.dma_start(out=bffn_sb, in_=bffn_in))
            out.append(e.dma_start(out=qnw_sb, in_=qnw_in))
            out.append(e.dma_start(out=knw_sb, in_=knw_in))
            out.append(e.dma_start(out=wbuf[2][:, 0:256], in_=dw_in))
            for c in range(8):
                out.append(e.dma_start(out=wqk[c], in_=wqk_in[c * 128:(c + 1) * 128, :]))
            return out
        sch.dma(ld, n=18)

        def consts(e):
            e.memset(ones1, 1.0)
            e.memset(ones128, 1.0)
            e.memset(c_ln2h, LN2H)
            e.memset(c_four, 4.0)
            return e.memset(c_0125, 0.125)
        sch.g(consts)

        def dload(e):
            return e.dma_start(out=d_w1, in_=dact_in)
        sch.dma(dload)
        def dtanh(e):
            e.activation(d_w2, d_w1, AF.Tanh, scale=c_four)
            return e.activation(d_w1, d_w2, AF.Exp, scale=c_ln2h)
        sch.s(dtanh, waits=sch.snap("d", "g"))
        def dmuls(e):
            last = None
            vals = TH_A + TH_B + TH_C + [0.5]
            tgts = thA + thB + thC + [half2d]
            for th, tgt in zip(vals[:-1], tgts[:-1]):
                e.raw.tensor_scalar(tgt, d_w1, float(th), None, op0=A.mult)
            return e.tensor_scalar(tgts[-1], d_w1, float(vals[-1]), None, op0=A.mult)
        sch.v(dmuls, waits=sch.snap("s"))

        for c in range(8):
            buf = wbuf[c % 2]
            def wl(e, c=c, buf=buf):
                return e.dma_start(out=buf, in_=wffn_in[c * 128:(c + 1) * 128, :])
            sch.dma(wl, waits=sch.snap("v", "s"))

            dwc = wbuf[2][:, c * 32 + 0:(c + 1) * 32]
            def wth(e, dwc=dwc):
                e.activation(w_t1, dwc, AF.Tanh, scale=c_four)
                return e.activation(w_twod, w_t1, AF.Exp, scale=c_ln2h)
            sch.s(wth, waits=sch.snap("d", "g"))

            def wstats(e, buf=buf):
                for i, th in enumerate(TH_A + TH_B + TH_C):
                    e.raw.tensor_scalar(w_th[i], w_twod, float(th), None, op0=A.mult)
                e.drain()
                e.tensor_reduce(q_am, buf.rearrange("p (b i) -> p b i", i=32),
                                mybir.AxisListType.X, A.max, apply_absolute_value=True)
                e.tensor_scalar(q_m, q_am, TINY, None, op0=A.max)
                e.memset(q_p, 2.0 ** -100)
                for k in (64, 32, 16, 8, 4, 2, 1):
                    e.tensor_scalar(q_t, q_p, float(2.0 ** k), None, op0=A.mult)
                    e.tensor_tensor(q_msk, q_m, q_t, op=A.is_ge)
                    e.tensor_tensor(q_tm, q_t, q_msk, op=A.mult)
                    e.tensor_tensor(q_p, q_p, q_tm, op=A.max)
                e.tensor_scalar(q_msk, q_am, 0.0, None, op0=A.is_gt)
                e.tensor_scalar(q_t, q_p, 0.25, 1.0, op0=A.mult, op1=A.subtract)
                e.tensor_tensor(q_tm, q_t, q_msk, op=A.mult)
                e.tensor_scalar(q_base, q_tm, 1.0, None, op0=A.add)
                e.reciprocal(q_rb, q_base)
                e.tensor_tensor(q_f, q_base, w_twod, op=A.mult)
                return e.tensor_scalar(q_f, q_f, 0.5, None, op0=A.mult)
            sch.v(wstats, waits=sch.snap("d", "s"))

            def wabs(e, buf=buf):
                e.activation(junk, buf, AF.Abs)
                return e.sign(xln2b[0], buf)
            sch.s(wabs, waits=sch.snap("d"))

            def wquant(e):
                qrB = p_sb[:, 0:D]
                qrC = p_sb[:, D:2 * D]
                bc = lambda t: t.rearrange("p (b o) -> p b o", o=1).to_broadcast([128, 32, 32])
                r3 = lambda t: t.rearrange("p (b i) -> p b i", i=32)
                e.tensor_tensor(r3(qaxs), r3(junk), bc(q_rb), op=A.mult)
                e.tensor_tensor(r3(qrA), r3(qaxs), bc(w_th[0]), op=A.is_ge)
                for i in (1, 2, 3):
                    e.tensor_tensor(r3(qrC), r3(qaxs), bc(w_th[i]), op=A.is_ge)
                    e.tensor_tensor(qrA, qrA, qrC, op=A.add)
                e.tensor_tensor(r3(qrB), r3(qaxs), bc(w_th[4]), op=A.is_ge)
                e.tensor_tensor(r3(qrC), r3(qaxs), bc(w_th[5]), op=A.is_ge)
                e.tensor_tensor(qrB, qrB, qrC, op=A.add)
                e.tensor_tensor(r3(qrC), r3(qaxs), bc(w_th[6]), op=A.is_ge)
                e.scalar_tensor_tensor(qrC, qrC, 2.0, qrB, op0=A.mult, op1=A.add)
                e.scalar_tensor_tensor(qrC, qrC, 2.0, qrA, op0=A.mult, op1=A.add)
                e.tensor_tensor(r3(qrB), r3(xln2b[0]), bc(q_f), op=A.mult)
                return e.tensor_tensor(x_sb2[0], qrC, qrB, op=A.mult)
            sch.v(wquant, waits=sch.snap("s"))

            def wtr(e, c=c):
                out = []
                for r in range(8):
                    out.append(e.transpose(ps_med[:, r * 128:(r + 1) * 128],
                                           x_sb2[0][:, r * 128:(r + 1) * 128], ident))
                return out[-1]
            sch.t(wtr, waits=sch.snap("v"))

            def wev(e, c=c):
                for r in range(7):
                    e.raw.tensor_copy(wqt[r][:, c * 128:(c + 1) * 128],
                                      ps_med[:, r * 128:(r + 1) * 128])
                return e.tensor_copy(wqt[7][:, c * 128:(c + 1) * 128],
                                     ps_med[:, 7 * 128:8 * 128])
            sch.v(wev, waits=sch.snap("t"))

    def emit_ln(x_ap, out_ap, front=False, extra=()):
        if front:
            vs, ss_, jk = "fv", "fs", junk2
            b_sum, b_ssq, b_m, b_t0, b_t1, b_var, b_r0, b_rstd = (
                f_sum, f_ssq, f_m, f_t0, f_t1, f_var, f_r0, f_rstd)
        else:
            vs, ss_, jk = "v", "s", junk
            b_sum, b_ssq, b_m, b_t0, b_t1, b_var, b_r0, b_rstd = (
                l_sum, l_ssq, l_m, l_t0, l_t1, l_var, l_r0, l_rstd)

        def vsum(e):
            return e.reduce_sum(b_sum, x_ap, mybir.AxisListType.X)
        sch.v(vsum, waits=list(extra), sem=vs)

        def ssq(e):
            return e.activation(jk, x_ap, AF.Square, accum_out=b_ssq)
        sch.s(ssq, waits=list(extra) + ([] if front else sch.snap("v")), sem=ss_)

        def stats(e):
            e.tensor_scalar(b_m, b_sum, 1.0 / D, None, op0=A.mult)
            e.tensor_scalar(b_t0, b_ssq, 1.0 / D, None, op0=A.mult)
            e.tensor_tensor(b_t1, b_m, b_m, op=A.mult)
            e.tensor_tensor(b_var, b_t0, b_t1, op=A.subtract)
            e.tensor_scalar(b_var, b_var, LN_EPS, None, op0=A.add)
            return e.reciprocal(b_t0, b_var)
        sch.v(stats, waits=[sch.tok(ss_)], sem=vs)

        def sq(e):
            return e.activation(b_r0, b_t0, AF.Sqrt)
        sch.s(sq, waits=[sch.tok(vs)], sem=ss_)

        def fin(e):
            e.tensor_tensor(b_t1, b_r0, b_r0, op=A.mult)
            e.tensor_tensor(b_t1, b_t1, b_var, op=A.mult)
            e.tensor_scalar(b_t1, b_t1, -0.5, 1.5, op0=A.mult, op1=A.add)
            e.tensor_tensor(b_rstd, b_r0, b_t1, op=A.mult)
            return e.tensor_scalar(out_ap, x_ap, b_m, b_rstd,
                                   op0=A.subtract, op1=A.mult)
        sch.v(fin, waits=[sch.tok(ss_)] + sch.snap("o"), sem=vs)

    def emit_transpose8(src_ap, ps, dst_ap, front=False, extra=()):
        ts_, vs = ("ft", "fv") if front else ("t", "v")

        def tr(e):
            last = None
            for r in range(8):
                last = e.transpose(ps[:, r * 128:(r + 1) * 128],
                                   src_ap[:, r * 128:(r + 1) * 128], ident)
            return last
        wt = list(extra) if front else sch.snap("v", "s", "g", "d")
        sch.t(tr, waits=wt, sem=ts_)

        def ev(e):
            return e.tensor_copy(dst_ap, ps)
        sch.v(ev, waits=[sch.tok(ts_)], sem=vs)

    def emit_quant_act(x_ap, out_ap, qrB, qrC):
        r3 = lambda t: t.rearrange("p (b i) -> p b i", i=32)
        bc = lambda t: t.rearrange("p (b o) -> p b o", o=1).to_broadcast([128, 32, 32])

        def vstat(e):
            e.tensor_reduce(q_am, r3(x_ap), mybir.AxisListType.X, A.max,
                            apply_absolute_value=True)
            e.tensor_scalar(q_m, q_am, TINY, None, op0=A.max)
            e.memset(q_p, 2.0 ** -100)
            for k in (64, 32, 16, 8, 4, 2, 1):
                e.tensor_scalar(q_t, q_p, float(2.0 ** k), None, op0=A.mult)
                e.tensor_tensor(q_msk, q_m, q_t, op=A.is_ge)
                e.tensor_tensor(q_tm, q_t, q_msk, op=A.mult)
                e.tensor_tensor(q_p, q_p, q_tm, op=A.max)
            e.tensor_scalar(q_msk, q_am, 0.0, None, op0=A.is_gt)
            e.tensor_scalar(q_t, q_p, 0.25, 1.0, op0=A.mult, op1=A.subtract)
            e.tensor_tensor(q_tm, q_t, q_msk, op=A.mult)
            e.tensor_scalar(q_base, q_tm, 1.0, None, op0=A.add)
            e.reciprocal(q_rb, q_base)
            return e.tensor_tensor(q_f, q_base, half2d.to_broadcast([128, 32]), op=A.mult)
        sch.v(vstat, waits=sch.snap("s", "t", "g", "d"))

        def sabs2(e):
            e.activation(junk, x_ap, AF.Abs)
            return e.sign(xpb, x_ap)
        sch.s(sabs2, waits=sch.snap("v", "t", "g", "d"))

        def vaxs(e):
            return e.tensor_tensor(r3(qaxs), r3(junk), bc(q_rb), op=A.mult)
        sch.v(vaxs, waits=sch.snap("s"))

        def vA(e):
            e.tensor_scalar(qrA, qaxs, thA[0], None, op0=A.is_ge)
            for i in (1, 2, 3):
                e.scalar_tensor_tensor(qrA, qaxs, thA[i], qrA,
                                       op0=A.is_ge, op1=A.add)
            e.tensor_scalar(qrB, qaxs, thB[0], None, op0=A.is_ge)
            e.scalar_tensor_tensor(qrB, qaxs, thB[1], qrB, op0=A.is_ge, op1=A.add)
            e.tensor_scalar(qrC, qaxs, thC[0], None, op0=A.is_ge)
            return e.scalar_tensor_tensor(qrC, qrC, 2.0, qrB, op0=A.mult, op1=A.add)
        sch.v(vA, waits=sch.snap("t"))

        def vfin(e):
            e.scalar_tensor_tensor(qrC, qrC, 2.0, qrA, op0=A.mult, op1=A.add)
            e.tensor_tensor(r3(qrB), r3(xpb), bc(q_f), op=A.mult)
            return e.tensor_tensor(out_ap, qrC, qrB, op=A.mult)
        sch.v(vfin, waits=sch.snap("s"))

    def stream_w(src_ap, dc, buf, extra_waits=()):
        def wl(e):
            return e.dma_start(out=buf, in_=src_ap[dc * 128:(dc + 1) * 128, :])
        sch.dma(wl, waits=list(extra_waits), sem="w")

    wctr = [0]

    def next_wbuf():
        b = wbuf[wctr[0] % 5]
        wctr[0] += 1
        return b

    front_tok = {}

    def p1_front_ln(i):
        p = i % 2
        row0 = i * 128

        def ldx(e):
            return e.dma_start(out=x_sb2[p], in_=x_in[row0:row0 + 128, :])
        sch.dma(ldx, waits=sch.snap("v", "s", "t", "g", "o"))
        dtok = sch.tok("d")
        emit_ln(x_sb2[p], xln2b[p], front=True, extra=[dtok])

    def p1_front_tr(i):
        p = i % 2
        emit_transpose8(xln2b[p], ps_aux, xlnT2[p], front=True,
                        extra=[sch.tok("fv")])
        front_tok[i] = sch.tok("fv")

    def p1_back(i, next_tr=None):
        p = i % 2
        row0 = i * 128
        x_sb, xln, xlnT = x_sb2[p], xln2b[p], xlnT2[p]
        u_sb = x_sb
        qrB = p_sb[:, 0:D]
        qrC = p_sb[:, D:2 * D]

        def qkmm(e):
            last = None
            for oc in range(16):
                for dc in range(8):
                    last = e.matmul(ps_big[:, oc * 128:(oc + 1) * 128],
                                    wqk[dc][:, oc * 128:(oc + 1) * 128],
                                    xlnT[:, dc * 128:(dc + 1) * 128],
                                    start=(dc == 0), stop=(dc == 7))
            return last
        sch.t(qkmm, waits=[front_tok[i]] + sch.snap("v", "d"))

        for dc in range(8):
            wb = next_wbuf()
            stream_w(wv_in, dc, wb, extra_waits=sch.snap("t"))
            def vmm(e, dc=dc, wb=wb):
                last = None
                for ob in range(2):
                    last = e.matmul(ps_med[:, ob * 512:(ob + 1) * 512],
                                    xlnT[:, dc * 128:(dc + 1) * 128],
                                    wb[:, ob * 512:(ob + 1) * 512],
                                    start=(dc == 0), stop=False)
                return last
            sch.t(vmm, waits=sch.snap("w", "v") if dc == 0 else sch.snap("w"))
        def vbias(e):
            e.matmul(ps_med[:, 0:512], ones1, bv_sb[:, 0:512], start=False, stop=True)
            return e.matmul(ps_med[:, 512:1024], ones1, bv_sb[:, 512:1024],
                            start=False, stop=True)
        sch.t(vbias)

        if next_tr is not None:
            next_tr()

        def evqk(e):
            for oc in range(16):
                e.raw.tensor_scalar(qkT[:, oc * 128:(oc + 1) * 128],
                                    ps_big[:, oc * 128:(oc + 1) * 128],
                                    bqk_sb[:, oc:oc + 1], None, op0=A.add)
            return e.tensor_copy(v_sb, ps_med)
        sch.v(evqk, waits=sch.snap("t"))

        def rebase(e):
            out = []
            for oc in range(16):
                out.append(e.dma_start(out=qkOdd[0:64, oc * 128:(oc + 1) * 128],
                                       in_=qkT[64:128, oc * 128:(oc + 1) * 128]))
            return out
        sch.dma(rebase, waits=sch.snap("v"), n=16)

        def qh(h):
            oc = h // 2
            src = qkT if h % 2 == 0 else qkOdd
            return src[0:64, oc * 128:(oc + 1) * 128]

        def kh(h):
            oc = 8 + h // 2
            src = qkT if h % 2 == 0 else qkOdd
            return src[0:64, oc * 128:(oc + 1) * 128]

        def smm(e):
            last = None
            for h in range(16):
                last = e.matmul(ps_big[:, h * 128:(h + 1) * 128], qh(h), kh(h),
                                start=True, stop=True)
            return last
        sch.t(smm, waits=sch.snap("d", "v"))

        def sm1(e):
            e.tensor_reduce(a_hm, ps_big.rearrange("p (h j) -> p h j", j=128),
                            mybir.AxisListType.X, A.max)
            e.reduce_max(a_cm, a_hm, mybir.AxisListType.X)
            return e.tensor_scalar(a_negc, a_cm, -0.125, None, op0=A.mult)
        sch.v(sm1, waits=sch.snap("t"))

        def sexp(e):
            return e.activation(p_sb, ps_big, AF.Exp, bias=a_negc, scale=c_0125)
        sch.s(sexp, waits=sch.snap("v"))

        def smask(e):
            e.tensor_tensor(p_sb.rearrange("p (h j) -> p h j", j=128),
                            p_sb.rearrange("p (h j) -> p h j", j=128),
                            mask.rearrange("p (o j) -> p o j", o=1)
                                .to_broadcast([128, 16, 128]),
                            op=A.mult)
            e.tensor_reduce(a_rs, p_sb.rearrange("p (h j) -> p h j", j=128),
                            mybir.AxisListType.X, A.add)
            return e.reciprocal(a_rr, a_rs)
        sch.v(smask, waits=sch.snap("s"))

        def ptr(e):
            last = None
            for h in range(16):
                last = e.transpose(ps_big[:, h * 128:(h + 1) * 128],
                                   p_sb[:, h * 128:(h + 1) * 128], ident)
            return last
        sch.t(ptr, waits=sch.snap("v"))

        def ptev(e):
            return e.tensor_copy(qkT, ps_big)
        sch.v(ptev, waits=sch.snap("t"))

        def avmm(e):
            last = None
            for h in range(16):
                last = e.matmul(ps_med[:, h * 64:(h + 1) * 64],
                                qkT[:, h * 128:(h + 1) * 128],
                                v_sb[:, h * 64:(h + 1) * 64],
                                start=True, stop=True)
            return last
        sch.t(avmm, waits=sch.snap("v"))

        def avev(e):
            return e.tensor_tensor(
                av_sb.rearrange("p (h j) -> p h j", j=64),
                ps_med.rearrange("p (h j) -> p h j", j=64),
                a_rr.rearrange("p (h o) -> p h o", o=1).to_broadcast([128, 16, 64]),
                op=A.mult)
        sch.v(avev, waits=sch.snap("t"))

        emit_transpose8(av_sb, ps_big[:, 0:D], avT)

        for dc in range(8):
            wb = next_wbuf()
            stream_w(wpr_in, dc, wb, extra_waits=sch.snap("t"))
            def pmm(e, dc=dc, wb=wb):
                last = None
                for ob in range(2):
                    last = e.matmul(ps_med[:, ob * 512:(ob + 1) * 512],
                                    avT[:, dc * 128:(dc + 1) * 128],
                                    wb[:, ob * 512:(ob + 1) * 512],
                                    start=(dc == 0), stop=False)
                return last
            sch.t(pmm, waits=sch.snap("w", "v") if dc == 0 else sch.snap("w"))
        def pbias(e):
            e.matmul(ps_med[:, 0:512], ones1, bpr_sb[:, 0:512], start=False, stop=True)
            return e.matmul(ps_med[:, 512:1024], ones1, bpr_sb[:, 512:1024],
                            start=False, stop=True)
        sch.t(pbias)

        def resid1(e):
            return e.tensor_tensor(x1_sb, ps_med, x_sb, op=A.add)
        sch.v(resid1, waits=sch.snap("t"))

        emit_ln(x1_sb, xln)
        emit_quant_act(xln, u_sb, qrB, qrC)
        emit_transpose8(u_sb, ps_big[:, 0:D], uT)

        def fmm(e):
            last = None
            for ob in range(2):
                for dc in range(8):
                    last = e.matmul(ps_med[:, ob * 512:(ob + 1) * 512],
                                    uT[:, dc * 128:(dc + 1) * 128],
                                    wqt[dc][:, ob * 512:(ob + 1) * 512],
                                    start=(dc == 0), stop=False)
                last = e.matmul(ps_med[:, ob * 512:(ob + 1) * 512], ones1,
                                bffn_sb[:, ob * 512:(ob + 1) * 512],
                                start=False, stop=True)
            return last
        sch.t(fmm, waits=sch.snap("v"))

        def resid2(e):
            return e.tensor_tensor(xln, ps_med, x1_sb, op=A.add)
        sch.v(resid2, waits=sch.snap("t"))

        def stx2(e):
            return e.dma_start(out=x2d[row0:row0 + 128, :], in_=xln)
        sch.dma(stx2, waits=sch.snap("v"), sem="o")

        if upto == "p1":
            def sty(e):
                return e.dma_start(out=y_out[row0:row0 + 128, :], in_=xln)
            sch.dma(sty, waits=sch.snap("v"), sem="o")

    x2d_bm = x2d.rearrange("(s b) d -> b s d", b=B)

    def p2_front_ln(j):
        p = j % 2
        b, s0 = j // 8, (j % 8) * 128

        def ldx(e):
            return e.dma_start(out=x_sb2[p], in_=x2d_bm[b, s0:s0 + 128, :])
        sch.dma(ldx, waits=sch.snap("v", "s", "t", "g", "o"))
        dtok = sch.tok("d")
        emit_ln(x_sb2[p], xln2b[p], front=True, extra=[dtok])

    def p2_front_tr(j):
        p = j % 2
        emit_transpose8(xln2b[p], ps_aux, xlnT2[p], front=True,
                        extra=[sch.tok("fv")])
        front_tok[j] = sch.tok("fv")

    def p2_back(j, next_tr=None):
        p = j % 2
        b = j // 8
        xlnT = xlnT2[p]
        k_ps = ps_big[:, 0:1024]
        v_ps = ps_big[:, 1024:2048]

        first = True
        for dc in range(8):
            bufs = []
            for part in range(3):
                wb = next_wbuf()
                bufs.append(wb)
                stream_w(wqb_in.rearrange("d (pp o) -> pp d o", pp=3)[part], dc,
                         wb, extra_waits=sch.snap("t"))
            def qmm(e, dc=dc, bufs=bufs):
                tgts = [ps_med, k_ps, v_ps]
                last = None
                for part in range(3):
                    for ob in range(2):
                        last = e.matmul(tgts[part][:, ob * 512:(ob + 1) * 512],
                                        xlnT[:, dc * 128:(dc + 1) * 128],
                                        bufs[part][:, ob * 512:(ob + 1) * 512],
                                        start=(dc == 0), stop=(dc == 7))
                return last
            w = sch.snap("w")
            if first:
                w = w + [front_tok[j]] + sch.snap("v")
                first = False
            sch.t(qmm, waits=w)

        if next_tr is not None:
            next_tr()

        def rms(ps, wrep, dst):
            def s1(e):
                return e.activation(junk, ps, AF.Square, accum_out=r_ssq)
            sch.s(s1, waits=sch.snap("t", "v"))
            def v1(e):
                e.tensor_scalar(r_t0, r_ssq, 1.0 / D, RMS_EPS, op0=A.mult, op1=A.add)
                return e.reciprocal(r_t1, r_t0)
            sch.v(v1, waits=sch.snap("s"))
            def s2(e):
                return e.activation(r_r0, r_t1, AF.Sqrt)
            sch.s(s2, waits=sch.snap("v"))
            def v2(e):
                e.tensor_tensor(r_rq, r_r0, r_r0, op=A.mult)
                e.tensor_tensor(r_rq, r_rq, r_t0, op=A.mult)
                e.tensor_scalar(r_rq, r_rq, -0.5, 1.5, op0=A.mult, op1=A.add)
                e.tensor_tensor(r_rq, r_r0, r_rq, op=A.mult)
                e.tensor_scalar(junk, ps, r_rq, None, op0=A.mult)
                e.tensor_tensor(junk, junk, wrep, op=A.mult)
                return e.tensor_scalar(dst, junk, 0.0, None, op0=A.max)
            sch.v(v2, waits=sch.snap("s"))

        qn_ap = qkT[:, 0:1024]
        kn_ap = qkT[:, 1024:2048]
        rms(ps_med, qnw_sb, qn_ap)
        rms(k_ps, knw_sb, kn_ap)
        def vev(e):
            return e.tensor_copy(v_sb, v_ps)
        sch.v(vev, waits=sch.snap("t"))

        def kvmm(e):
            last = None
            for h in range(16):
                last = e.matmul(ps_med[0:64, h * 64:(h + 1) * 64],
                                kn_ap[:, h * 64:(h + 1) * 64],
                                v_sb[:, h * 64:(h + 1) * 64],
                                start=True, stop=True)
            for h in range(16):
                last = e.matmul(ps_big[0:64, h:h + 1],
                                kn_ap[:, h * 64:(h + 1) * 64], ones128,
                                start=True, stop=True)
            return last
        sch.t(kvmm, waits=sch.snap("v"))

        def kvacc(e):
            e.raw.tensor_tensor(kvacc_ap(b), ps_med[0:64, 0:1024],
                                kvacc_ap(b), op=A.add)
            return e.tensor_tensor(ksacc_ap(b), ps_big[0:64, 0:16],
                                   ksacc_ap(b), op=A.add)
        sch.v(kvacc, waits=sch.snap("t"))

        def stqn(e):
            return e.dma_start(out=qnd[j * 128:(j + 1) * 128, :], in_=qn_ap)
        sch.dma(stqn, waits=sch.snap("v"), sem="o")

    def p3():
        def st(e):
            out = []
            for bb in range(B):
                out.append(e.dma_start(out=cc_in.ap()[:, bb * 1024:(bb + 1) * 1024],
                                       in_=kvacc_ap(bb)))
            out.append(e.dma_start(out=cc_in.ap()[:, 4096:4160],
                                   in_=wqk[2][0:64, 0:64]))
            return out
        sch.dma(st, waits=sch.snap("v"), n=5)

        def cc(e):
            return e.collective_compute(
                "AllReduce", A.add, replica_groups=[list(range(NCORES))],
                ins=[cc_in.ap().opt()], outs=[cc_out.ap().opt()])
        sch.emit("gpsimd", cc, waits=sch.snap("d"), inc=scc, n=1, amt=1)

        def ld(e):
            o1 = e.dma_start(out=wqk[3][0:64, :], in_=cc_out.ap()[:, 0:2048])
            o2 = e.dma_start(out=wqk[4][0:64, :], in_=cc_out.ap()[:, 2048:4096])
            o3 = e.dma_start(out=wqk[5][0:64, 0:64], in_=cc_out.ap()[:, 4096:4160])
            return [o1, o2, o3]
        sch.dma(ld, waits=sch.snap("c"), n=3)

    def p4_front_ln(j):
        p = j % 2
        b, s0 = j // 8, (j % 8) * 128
        qn_ap = p_sb[:, p * 1024:(p + 1) * 1024]

        def ldq(e):
            o1 = e.dma_start(out=qn_ap, in_=qnd[j * 128:(j + 1) * 128, :])
            o2 = e.dma_start(out=x_sb2[p], in_=x2d_bm[b, s0:s0 + 128, :])
            return [o1, o2]
        sch.dma(ldq, waits=sch.snap("v", "s", "t", "g", "o"), n=2)
        front_tok[(j, "d")] = sch.tok("d")

    def p4_front_tr(j):
        p = j % 2
        qn_ap = p_sb[:, p * 1024:(p + 1) * 1024]
        emit_transpose8(qn_ap, ps_aux, xlnT2[p], front=True,
                        extra=[sch.tok("fv"), front_tok[(j, "d")]])
        evtok = sch.tok("fv")

        def rebase(e):
            out = []
            for dc in range(8):
                out.append(e.dma_start(
                    out=qkOdd[0:64, p * 1024 + dc * 128:p * 1024 + (dc + 1) * 128],
                    in_=xlnT2[p][64:128, dc * 128:(dc + 1) * 128]))
            return out
        sch.dma(rebase, waits=[evtok] + sch.snap("t"), n=8)
        front_tok[j] = (evtok, sch.tok("d"))

    def p4_back(j, next_tr=None):
        p = j % 2
        b = j // 8
        x_sb, xln, xlnT = x_sb2[p], xln2b[p], xlnT2[p]
        u_sb = x_sb
        qrB = qkT[:, 0:D]
        qrC = qkT[:, D:2 * D]

        def qh(h):
            dc = h // 2
            if h % 2 == 0:
                return xlnT[0:64, dc * 128:(dc + 1) * 128]
            return qkOdd[0:64, p * 1024 + dc * 128:p * 1024 + (dc + 1) * 128]

        evtok, rbtok = front_tok[j]

        def attmm(e):
            last = None
            for h in range(16):
                last = e.matmul(ps_med[:, h * 64:(h + 1) * 64], qh(h),
                                kvg_kv(b, h), start=True, stop=True)
            for h in range(16):
                last = e.matmul(ps_big[:, h:h + 1], qh(h),
                                kvg_ks(b, h), start=True, stop=True)
            return last
        sch.t(attmm, waits=[evtok, rbtok] + sch.snap("v"))

        def nrm(e):
            e.tensor_scalar(nrm_sb, ps_big[:, 0:16], DEN_EPS, None, op0=A.add)
            e.reciprocal(nrr_sb, nrm_sb)
            return e.tensor_tensor(
                av_sb.rearrange("p (h j) -> p h j", j=64),
                ps_med.rearrange("p (h j) -> p h j", j=64),
                nrr_sb.rearrange("p (h o) -> p h o", o=1).to_broadcast([128, 16, 64]),
                op=A.mult)
        sch.v(nrm, waits=sch.snap("t"))

        emit_transpose8(av_sb, ps_big[:, 0:D], avT)

        for dc in range(8):
            wb = next_wbuf()
            stream_w(wout_in, dc, wb, extra_waits=sch.snap("t"))
            def omm(e, dc=dc, wb=wb):
                last = None
                for ob in range(2):
                    last = e.matmul(ps_med[:, ob * 512:(ob + 1) * 512],
                                    avT[:, dc * 128:(dc + 1) * 128],
                                    wb[:, ob * 512:(ob + 1) * 512],
                                    start=(dc == 0), stop=False)
                return last
            sch.t(omm, waits=sch.snap("w", "v") if dc == 0 else sch.snap("w"))
        def obias(e):
            e.matmul(ps_med[:, 0:512], ones1, bo_sb[:, 0:512], start=False, stop=True)
            return e.matmul(ps_med[:, 512:1024], ones1, bo_sb[:, 512:1024],
                            start=False, stop=True)
        sch.t(obias)

        if next_tr is not None:
            next_tr()

        def resid3(e):
            return e.tensor_tensor(x1_sb, ps_med, x_sb, op=A.add)
        sch.v(resid3, waits=sch.snap("t"))

        emit_ln(x1_sb, xln)
        emit_quant_act(xln, u_sb, qrB, qrC)
        emit_transpose8(u_sb, ps_big[:, 0:D], uT)

        def fmm(e):
            last = None
            for ob in range(2):
                for dc in range(8):
                    last = e.matmul(ps_med[:, ob * 512:(ob + 1) * 512],
                                    uT[:, dc * 128:(dc + 1) * 128],
                                    wqt[dc][:, ob * 512:(ob + 1) * 512],
                                    start=(dc == 0), stop=False)
                last = e.matmul(ps_med[:, ob * 512:(ob + 1) * 512], ones1,
                                bffn_sb[:, ob * 512:(ob + 1) * 512],
                                start=False, stop=True)
            return last
        sch.t(fmm, waits=sch.snap("v"))

        def resid4(e):
            return e.tensor_tensor(xln, ps_med, x1_sb, op=A.add)
        sch.v(resid4, waits=sch.snap("t"))

        def sty(e):
            return e.dma_start(out=y_out[j * 128:(j + 1) * 128, :], in_=xln)
        sch.dma(sty, waits=sch.snap("v"), sem="o")

    p0()
    p1_front_ln(0)
    p1_front_tr(0)
    for i in range(nt):
        if i + 1 < nt:
            p1_front_ln(i + 1)
            p1_back(i, next_tr=lambda i=i: p1_front_tr(i + 1))
        else:
            p1_back(i)

    if upto != "p1":
        def zacc(e):
            for bb in range(B):
                e.memset(kvacc_ap(bb), 0.0)
                e.memset(ksacc_ap(bb), 0.0)
            return e.memset(wqk[2][0:64, 0:64], 0.0)
        sch.g(zacc, waits=sch.snap("v", "t"))
        p2_front_ln(0)
        p2_front_tr(0)
        for j in range(nt):
            if j + 1 < nt:
                p2_front_ln(j + 1)
                p2_back(j, next_tr=lambda j=j: p2_front_tr(j + 1))
            else:
                p2_back(j)
        if upto not in ("p2", "p3"):
            p3()
            p4_front_ln(0)
            p4_front_tr(0)
            for j in range(nt):
                if j + 1 < nt:
                    p4_front_ln(j + 1)
                    p4_back(j, next_tr=lambda j=j: p4_front_tr(j + 1))
                else:
                    p4_back(j)
        elif upto == "p3":
            p3()

    o_final = so.n

    def fin(e):
        e.wait_ge(so.h, o_final)
        return None
    sch.emit("sync", fin, inc=None)

    with nc.Block() as block:
        @block.sync
        def _(e):
            for fn in sch.q["sync"]:
                fn(e)

        @block.vector
        def _(e):
            e = _Drainer(e)
            for fn in sch.q["vector"]:
                fn(e)

        @block.scalar
        def _(e):
            e = _Drainer(e)
            for fn in sch.q["scalar"]:
                fn(e)

        @block.tensor
        def _(e):
            for fn in sch.q["tensor"]:
                fn(e)

        @block.gpsimd
        def _(e):
            e = _Drainer(e)
            for fn in sch.q["gpsimd"]:
                fn(e)

    stack.close()
    return nc


def stage_inputs(inputs, core):
    x = inputs["L0_attn_a_x"]
    xs = x[:, core * S_CORE:(core + 1) * S_CORE, :]
    x_core = np.ascontiguousarray(np.transpose(xs, (1, 0, 2)).reshape(T_CORE, D))

    wqkv = inputs["L0_attn_a_Wqkv"]
    bqkv = inputs["L0_attn_a_bqkv"]
    dw = inputs["L0_ffn_delta_w"]

    return {
        "x": x_core,
        "wqk_t": np.ascontiguousarray(wqkv[0:2048, :].T),
        "bqk_pp": np.ascontiguousarray(bqkv[0:2048].reshape(16, 128).T),
        "wv_t": np.ascontiguousarray(wqkv[2048:3072, :].T),
        "bv_row": np.ascontiguousarray(bqkv[2048:3072][None, :]),
        "wproj_t": np.ascontiguousarray(inputs["L0_attn_a_Wproj"].T),
        "bproj_row": np.ascontiguousarray(inputs["L0_attn_a_bproj"][None, :]),
        "wffn": np.ascontiguousarray(inputs["L0_ffn_weight"]),
        "dw": np.ascontiguousarray(
            np.transpose(dw.reshape(8, 128, 32), (1, 0, 2)).reshape(128, 256)),
        "bffn_row": np.ascontiguousarray(inputs["L0_ffn_bias"][None, :]),
        "dact": np.ascontiguousarray(np.broadcast_to(
            inputs["L0_ffn_delta_act"].reshape(1, 1), (128, 1))),
        "wqkv_b": np.ascontiguousarray(inputs["L0_attn_b_W_qkv"]),
        "qnw_rep": np.ascontiguousarray(
            np.broadcast_to(inputs["L0_attn_b_qn_w"][None, :], (128, D))),
        "knw_rep": np.ascontiguousarray(
            np.broadcast_to(inputs["L0_attn_b_kn_w"][None, :], (128, D))),
        "wout": np.ascontiguousarray(inputs["L0_attn_b_W_out"]),
        "bout_row": np.ascontiguousarray(inputs["L0_attn_b_b_out"][None, :]),
        "ident": np.eye(128, dtype=np.float32),
        "mask": np.kron(np.eye(32, dtype=np.float32),
                        np.ones((4, 4), dtype=np.float32)),
    }


def kernel(**inputs):
    inputs = {k: np.asarray(v, dtype=np.float32) for k, v in inputs.items()}
    nc = build_program("all")
    in_maps = [stage_inputs(inputs, c) for c in range(NCORES)]
    res = run_bass_kernel_spmd(nc, in_maps, core_ids=list(range(NCORES)))
    out = np.empty((B, S, D), dtype=np.float32)
    for c in range(NCORES):
        yc = np.asarray(res.results[c]["y"], dtype=np.float32).reshape(B, S_CORE, D)
        out[:, c * S_CORE:(c + 1) * S_CORE, :] = yc
    return out


# version 09 (final)
# speedup vs baseline: 1.4464x; 1.3635x over previous
import math
import contextlib
import numpy as np
import concourse.bass as bass
import bass_rust
import concourse.mybir as mybir
from concourse.bass_utils import run_bass_kernel_spmd

F32 = mybir.dt.float32
A = mybir.AluOpType
AF = mybir.ActivationFunctionType

NCORES = 8
B, S, D = 4, 8192, 1024
S_CORE = S // NCORES
T_CORE = B * S_CORE
NT = T_CORE // 128
H, DH = 16, 64
LN_EPS = 1e-5
RMS_EPS = 1e-6
DEN_EPS = 1e-6
TINY = 1e-30
TH_A = [0.25, 0.75, 1.25, 1.75]
TH_B = [2.5, 3.5]
TH_C = [5.0]
LN2H = math.log(2.0) / 2.0

ENGINES = ("sync", "vector", "scalar", "tensor", "gpsimd")


class _Drainer:

    _PASS = {"wait_ge", "drain"}

    def __init__(self, e):
        self._e = e
        self.raw = e

    def __getattr__(self, name):
        f = getattr(self._e, name)
        if name in self._PASS or not callable(f):
            return f
        def wrapper(*a, **k):
            r = f(*a, **k)
            self._e.drain()
            return r
        return wrapper


class Sem:
    def __init__(self, handle):
        self.h = handle
        self.n = 0


class Sched:

    def __init__(self, nc, stack):
        self.nc = nc
        self.q = {e: [] for e in ENGINES}
        self.sem = {}
        for name in ("d", "v", "s", "t", "g", "o", "c", "w", "fv", "fs", "ft"):
            self.sem[name] = Sem(stack.enter_context(nc.semaphore("sem_" + name)))

    def emit(self, eng, builder, waits=(), inc=None, n=1, amt=1):
        frozen = tuple((s, c) for (s, c) in waits if c > 0)

        def fn(e):
            for s, c in frozen:
                e.wait_ge(s.h, c)
            out = builder(e)
            if inc is not None:
                instrs = out if isinstance(out, (list, tuple)) else [out]
                assert len(instrs) == n, f"expected {n} instrs, got {len(instrs)}"
                for ins in instrs:
                    ins.then_inc(inc.h, amt)

        self.q[eng].append(fn)
        if inc is not None:
            inc.n += n * amt

    def v(self, builder, waits=(), sem="v"):
        self.emit("vector", builder, waits, self.sem[sem])

    def s(self, builder, waits=(), sem="s"):
        self.emit("scalar", builder, waits, self.sem[sem])

    def t(self, builder, waits=(), sem="t"):
        self.emit("tensor", builder, waits, self.sem[sem])

    def tok(self, name):
        return (self.sem[name], self.sem[name].n)

    def g(self, builder, waits=()):
        self.emit("gpsimd", builder, waits, self.sem["g"])

    def dma(self, builder, waits=(), n=1, sem="d"):
        self.emit("sync", builder, waits, self.sem[sem], n=n, amt=16)

    def gdma(self, builder, waits=(), n=1, sem="g"):
        self.emit("gpsimd", builder, waits, self.sem[sem], n=n, amt=16)

    def snap(self, *names):
        return [(self.sem[k], self.sem[k].n) for k in names]


def build_program(upto="all", nt=NT, tap=None):
    nc = bass.Bass("TRN2", target_bir_lowering=False, debug=False,
                   detect_race_conditions=False, dynamic_dma_scratch_size=512)

    def din(name, shape):
        return nc.dram_tensor(name, shape, F32, kind="ExternalInput").ap()

    x_in = din("x", [T_CORE, D])
    wqk_in = din("wqk_t", [D, 2 * D])
    bqk_in = din("bqk_pp", [128, 16])
    wv_in = din("wv_t", [D, D])
    bv_in = din("bv_row", [1, D])
    wpr_in = din("wproj_t", [D, D])
    bpr_in = din("bproj_row", [1, D])
    wffn_in = din("wffn", [D, D])
    dw_in = din("dw", [128, 8 * 32])
    bffn_in = din("bffn_row", [1, D])
    dact_in = din("dact", [128, 1])
    wqb_in = din("wqkv_b", [D, 3 * D])
    qnw_in = din("qnw_rep", [128, D])
    knw_in = din("knw_rep", [128, D])
    wout_in = din("wout", [D, D])
    bout_in = din("bout_row", [1, D])
    ident_in = din("ident", [128, 128])
    mask_in = din("mask", [128, 128])

    y_out = nc.dram_tensor("y", [T_CORE, D], F32, kind="ExternalOutput").ap()

    x2d = nc.dram_tensor("x2d", [T_CORE, D], F32).ap()
    qnd = nc.dram_tensor("qnd", [T_CORE, D], F32).ap()
    cc_in = nc.dram_tensor("cc_in", [64, 4160], F32)
    cc_out = nc.dram_tensor("cc_out", [64, 4160], F32)

    def sbt(name, shape, dt=F32):
        return nc.alloc_sbuf_tensor(name, shape, dt).ap()

    wqk = [sbt(f"wqk{c}", [128, 2 * D]) for c in range(8)]
    wqt = [sbt(f"wqt{c}", [128, D]) for c in range(8)]
    wbuf = [sbt(f"wbuf{i}", [128, D]) for i in range(5)]

    x_sb2 = [sbt(f"x_sb{p}", [128, D]) for p in range(2)]
    xln2b = [sbt(f"xln{p}", [128, D]) for p in range(2)]
    xlnT2 = [sbt(f"xlnT{p}", [128, D]) for p in range(2)]
    qkT = sbt("qkT", [128, 2 * D])
    qkOdd = sbt("qkOdd", [64, 2 * D])
    v_sb = sbt("v_sb", [128, D])
    p_sb = sbt("p_sb", [128, 2 * D])
    av_sb = sbt("av_sb", [128, D])
    avT = sbt("avT", [128, D])
    x1_sb = sbt("x1_sb", [128, D])
    uT = sbt("uT", [128, D])
    junk = sbt("junk", [128, D])
    xpb = uT
    qaxs = av_sb
    qrA = avT

    ident = sbt("ident_sb", [128, 128])
    mask = sbt("mask_sb", [128, 128])
    bqk_sb = sbt("bqk_sb", [128, 16])
    brows = sbt("brows", [1, 4 * D])
    bv_sb = brows[:, 0:D]
    bpr_sb = brows[:, D:2 * D]
    bo_sb = brows[:, 2 * D:3 * D]
    bffn_sb = brows[:, 3 * D:4 * D]
    qnw_sb = sbt("qnw_sb", [128, D])
    knw_sb = sbt("knw_sb", [128, D])
    ones1 = sbt("ones1", [1, 128])
    ones128 = sbt("ones128", [128, 1])

    q_am = sbt("q_am", [128, 32])
    q_m = sbt("q_m", [128, 32])
    q_p = sbt("q_p", [128, 32])
    q_t = sbt("q_t", [128, 32])
    q_msk = sbt("q_msk", [128, 32])
    q_tm = sbt("q_tm", [128, 32])
    q_base = sbt("q_base", [128, 32])
    q_rb = sbt("q_rb", [128, 32])
    q_f = sbt("q_f", [128, 32])
    w_t1 = wbuf[4][:, 512:544]
    w_twod = wbuf[4][:, 544:576]
    w_th = [wbuf[4][:, 576 + 32 * i:608 + 32 * i] for i in range(7)]

    junk2 = sbt("junk2", [128, D])
    f_sum = sbt("f_sum", [128, 1])
    f_ssq = sbt("f_ssq", [128, 1])
    f_m = sbt("f_m", [128, 1])
    f_t0 = sbt("f_t0", [128, 1])
    f_t1 = sbt("f_t1", [128, 1])
    f_var = sbt("f_var", [128, 1])
    f_r0 = sbt("f_r0", [128, 1])
    f_rstd = sbt("f_rstd", [128, 1])
    l_sum = sbt("l_sum", [128, 1])
    l_ssq = sbt("l_ssq", [128, 1])
    l_m = sbt("l_m", [128, 1])
    l_t0 = sbt("l_t0", [128, 1])
    l_t1 = sbt("l_t1", [128, 1])
    l_var = sbt("l_var", [128, 1])
    l_r0 = sbt("l_r0", [128, 1])
    l_rstd = sbt("l_rstd", [128, 1])

    a_hm = sbt("a_hm", [128, 16])
    a_cm = sbt("a_cm", [128, 1])
    a_cmax = sbt("a_cmax", [128, 1])
    a_negc = sbt("a_negc", [128, 1])
    a_rs = sbt("a_rs", [128, 16])
    a_rr = sbt("a_rr", [128, 16])

    thA = [sbt(f"thA{i}", [128, 1]) for i in range(4)]
    thB = [sbt(f"thB{i}", [128, 1]) for i in range(2)]
    thC = [sbt(f"thC{i}", [128, 1]) for i in range(1)]
    half2d = sbt("half2d", [128, 1])
    d_w1 = sbt("d_w1", [128, 1])
    d_w2 = sbt("d_w2", [128, 1])
    c_ln2h = sbt("c_ln2h", [128, 1])
    c_four = sbt("c_four", [128, 1])
    c_0125 = sbt("c_0125", [128, 1])

    r_ssq = sbt("r_ssq", [128, 1])
    r_r0 = sbt("r_r0", [128, 1])
    r_t0 = sbt("r_t0", [128, 1])
    r_t1 = sbt("r_t1", [128, 1])
    r_rq = sbt("r_rq", [128, 1])

    def kvacc_ap(b):
        return wqk[b // 2][0:64, (b % 2) * 1024:(b % 2 + 1) * 1024]

    def ksacc_ap(b):
        return wqk[2][0:64, b * 16:(b + 1) * 16]

    def kvg_kv(b, h):
        col0 = b * 1024 + h * 64
        return wqk[3 + col0 // 2048][0:64, col0 % 2048:col0 % 2048 + 64]

    def kvg_ks(b, h):
        return wqk[5][0:64, b * 16 + h:b * 16 + h + 1]

    nrm_sb = sbt("nrm_sb", [128, 16])
    nrr_sb = sbt("nrr_sb", [128, 16])

    ps_big = nc.alloc_psum_tensor("ps_big", [128, 2 * D], F32).ap()
    ps_med = nc.alloc_psum_tensor("ps_med", [128, D], F32).ap()
    ps_aux = nc.alloc_psum_tensor("ps_aux", [128, D], F32).ap()

    stack = contextlib.ExitStack()
    sch = Sched(nc, stack)
    sd, sv, ssem, stn, sg = (sch.sem[k] for k in "dvstg")
    so, scc = sch.sem["o"], sch.sem["c"]

    def p0():
        def ld(e):
            out = []
            out.append(e.dma_start(out=ident, in_=ident_in))
            out.append(e.dma_start(out=mask, in_=mask_in))
            out.append(e.dma_start(out=bqk_sb, in_=bqk_in))
            out.append(e.dma_start(out=bv_sb, in_=bv_in))
            out.append(e.dma_start(out=bpr_sb, in_=bpr_in))
            out.append(e.dma_start(out=bo_sb, in_=bout_in))
            out.append(e.dma_start(out=bffn_sb, in_=bffn_in))
            out.append(e.dma_start(out=qnw_sb, in_=qnw_in))
            out.append(e.dma_start(out=knw_sb, in_=knw_in))
            out.append(e.dma_start(out=wbuf[2][:, 0:256], in_=dw_in))
            for c in range(8):
                out.append(e.dma_start(out=wqk[c], in_=wqk_in[c * 128:(c + 1) * 128, :]))
            return out
        sch.dma(ld, n=18)

        def consts(e):
            e.memset(ones1, 1.0)
            e.memset(ones128, 1.0)
            e.memset(c_ln2h, LN2H)
            e.memset(c_four, 4.0)
            return e.memset(c_0125, 0.125)
        sch.g(consts)

        def dload(e):
            return e.dma_start(out=d_w1, in_=dact_in)
        sch.dma(dload)
        def dtanh(e):
            e.activation(d_w2, d_w1, AF.Tanh, scale=c_four)
            return e.activation(d_w1, d_w2, AF.Exp, scale=c_ln2h)
        sch.s(dtanh, waits=sch.snap("d", "g"))
        def dmuls(e):
            last = None
            vals = TH_A + TH_B + TH_C + [0.5]
            tgts = thA + thB + thC + [half2d]
            for th, tgt in zip(vals[:-1], tgts[:-1]):
                e.raw.tensor_scalar(tgt, d_w1, float(th), None, op0=A.mult)
            return e.tensor_scalar(tgts[-1], d_w1, float(vals[-1]), None, op0=A.mult)
        sch.v(dmuls, waits=sch.snap("s"))

        for c in range(8):
            buf = wbuf[c % 2]
            def wl(e, c=c, buf=buf):
                return e.dma_start(out=buf, in_=wffn_in[c * 128:(c + 1) * 128, :])
            sch.dma(wl, waits=sch.snap("v", "s"))

            dwc = wbuf[2][:, c * 32 + 0:(c + 1) * 32]
            def wth(e, dwc=dwc):
                e.activation(w_t1, dwc, AF.Tanh, scale=c_four)
                return e.activation(w_twod, w_t1, AF.Exp, scale=c_ln2h)
            sch.s(wth, waits=sch.snap("d", "g"))

            def wstats(e, buf=buf):
                for i, th in enumerate(TH_A + TH_B + TH_C):
                    e.raw.tensor_scalar(w_th[i], w_twod, float(th), None, op0=A.mult)
                e.drain()
                e.tensor_reduce(q_am, buf.rearrange("p (b i) -> p b i", i=32),
                                mybir.AxisListType.X, A.max, apply_absolute_value=True)
                e.tensor_scalar(q_m, q_am, TINY, None, op0=A.max)
                e.memset(q_p, 2.0 ** -100)
                for k in (64, 32, 16, 8, 4, 2, 1):
                    e.tensor_scalar(q_t, q_p, float(2.0 ** k), None, op0=A.mult)
                    e.tensor_tensor(q_msk, q_m, q_t, op=A.is_ge)
                    e.tensor_tensor(q_tm, q_t, q_msk, op=A.mult)
                    e.tensor_tensor(q_p, q_p, q_tm, op=A.max)
                e.tensor_scalar(q_msk, q_am, 0.0, None, op0=A.is_gt)
                e.tensor_scalar(q_t, q_p, 0.25, 1.0, op0=A.mult, op1=A.subtract)
                e.tensor_tensor(q_tm, q_t, q_msk, op=A.mult)
                e.tensor_scalar(q_base, q_tm, 1.0, None, op0=A.add)
                e.reciprocal(q_rb, q_base)
                e.tensor_tensor(q_f, q_base, w_twod, op=A.mult)
                return e.tensor_scalar(q_f, q_f, 0.5, None, op0=A.mult)
            sch.v(wstats, waits=sch.snap("d", "s"))

            def wabs(e, buf=buf):
                e.activation(junk, buf, AF.Abs)
                return e.sign(xln2b[0], buf)
            sch.s(wabs, waits=sch.snap("d"))

            def wquant(e):
                qrB = p_sb[:, 0:D]
                qrC = p_sb[:, D:2 * D]
                bc = lambda t: t.rearrange("p (b o) -> p b o", o=1).to_broadcast([128, 32, 32])
                r3 = lambda t: t.rearrange("p (b i) -> p b i", i=32)
                e.tensor_tensor(r3(qaxs), r3(junk), bc(q_rb), op=A.mult)
                e.tensor_tensor(r3(qrA), r3(qaxs), bc(w_th[0]), op=A.is_ge)
                for i in (1, 2, 3):
                    e.tensor_tensor(r3(qrC), r3(qaxs), bc(w_th[i]), op=A.is_ge)
                    e.tensor_tensor(qrA, qrA, qrC, op=A.add)
                e.tensor_tensor(r3(qrB), r3(qaxs), bc(w_th[4]), op=A.is_ge)
                e.tensor_tensor(r3(qrC), r3(qaxs), bc(w_th[5]), op=A.is_ge)
                e.tensor_tensor(qrB, qrB, qrC, op=A.add)
                e.tensor_tensor(r3(qrC), r3(qaxs), bc(w_th[6]), op=A.is_ge)
                e.scalar_tensor_tensor(qrC, qrC, 2.0, qrB, op0=A.mult, op1=A.add)
                e.scalar_tensor_tensor(qrC, qrC, 2.0, qrA, op0=A.mult, op1=A.add)
                e.tensor_tensor(r3(qrB), r3(xln2b[0]), bc(q_f), op=A.mult)
                return e.tensor_tensor(x_sb2[0], qrC, qrB, op=A.mult)
            sch.v(wquant, waits=sch.snap("s"))

            def wtr(e, c=c):
                out = []
                for r in range(8):
                    out.append(e.transpose(ps_med[:, r * 128:(r + 1) * 128],
                                           x_sb2[0][:, r * 128:(r + 1) * 128], ident))
                return out[-1]
            sch.t(wtr, waits=sch.snap("v"))

            def wev(e, c=c):
                for r in range(7):
                    e.raw.tensor_copy(wqt[r][:, c * 128:(c + 1) * 128],
                                      ps_med[:, r * 128:(r + 1) * 128])
                return e.tensor_copy(wqt[7][:, c * 128:(c + 1) * 128],
                                     ps_med[:, 7 * 128:8 * 128])
            sch.v(wev, waits=sch.snap("t"))

    def emit_ln(x_ap, out_ap, front=False, extra=()):
        if front:
            vs, ss_, jk = "fv", "fs", junk2
            b_sum, b_ssq, b_m, b_t0, b_t1, b_var, b_r0, b_rstd = (
                f_sum, f_ssq, f_m, f_t0, f_t1, f_var, f_r0, f_rstd)
        else:
            vs, ss_, jk = "v", "s", junk
            b_sum, b_ssq, b_m, b_t0, b_t1, b_var, b_r0, b_rstd = (
                l_sum, l_ssq, l_m, l_t0, l_t1, l_var, l_r0, l_rstd)

        def vsum(e):
            return e.reduce_sum(b_sum, x_ap, mybir.AxisListType.X)
        sch.v(vsum, waits=list(extra), sem=vs)

        def ssq(e):
            return e.activation(jk, x_ap, AF.Square, accum_out=b_ssq)
        sch.s(ssq, waits=list(extra) + ([] if front else sch.snap("v")), sem=ss_)

        def stats(e):
            e.tensor_scalar(b_m, b_sum, 1.0 / D, None, op0=A.mult)
            e.tensor_scalar(b_t0, b_ssq, 1.0 / D, None, op0=A.mult)
            e.tensor_tensor(b_t1, b_m, b_m, op=A.mult)
            e.tensor_tensor(b_var, b_t0, b_t1, op=A.subtract)
            e.tensor_scalar(b_var, b_var, LN_EPS, None, op0=A.add)
            return e.reciprocal(b_t0, b_var)
        sch.v(stats, waits=[sch.tok(ss_)], sem=vs)

        def sq(e):
            return e.activation(b_r0, b_t0, AF.Sqrt)
        sch.s(sq, waits=[sch.tok(vs)], sem=ss_)

        def fin(e):
            e.tensor_tensor(b_t1, b_r0, b_r0, op=A.mult)
            e.tensor_tensor(b_t1, b_t1, b_var, op=A.mult)
            e.tensor_scalar(b_t1, b_t1, -0.5, 1.5, op0=A.mult, op1=A.add)
            e.tensor_tensor(b_rstd, b_r0, b_t1, op=A.mult)
            return e.tensor_scalar(out_ap, x_ap, b_m, b_rstd,
                                   op0=A.subtract, op1=A.mult)
        sch.v(fin, waits=[sch.tok(ss_)] + sch.snap("o"), sem=vs)

    def emit_transpose8(src_ap, ps, dst_ap, front=False, extra=None):
        ts_, vs = ("ft", "fv") if front else ("t", "v")

        def tr(e):
            last = None
            for r in range(8):
                last = e.transpose(ps[:, r * 128:(r + 1) * 128],
                                   src_ap[:, r * 128:(r + 1) * 128], ident)
            return last
        wt = list(extra) if extra is not None else sch.snap("v", "s", "g", "d")
        sch.t(tr, waits=wt, sem=ts_)

        def ev(e):
            return e.tensor_copy(dst_ap, ps)
        sch.v(ev, waits=[sch.tok(ts_)], sem=vs)

    def emit_quant_act(x_ap, out_ap, qrB, qrC):
        r3 = lambda t: t.rearrange("p (b i) -> p b i", i=32)
        bc = lambda t: t.rearrange("p (b o) -> p b o", o=1).to_broadcast([128, 32, 32])

        def vstat(e):
            e.tensor_reduce(q_am, r3(x_ap), mybir.AxisListType.X, A.max,
                            apply_absolute_value=True)
            e.tensor_scalar(q_m, q_am, TINY, None, op0=A.max)
            e.memset(q_p, 2.0 ** -100)
            for k in (64, 32, 16, 8, 4, 2, 1):
                e.tensor_scalar(q_t, q_p, float(2.0 ** k), None, op0=A.mult)
                e.tensor_tensor(q_msk, q_m, q_t, op=A.is_ge)
                e.tensor_tensor(q_tm, q_t, q_msk, op=A.mult)
                e.tensor_tensor(q_p, q_p, q_tm, op=A.max)
            e.tensor_scalar(q_msk, q_am, 0.0, None, op0=A.is_gt)
            e.tensor_scalar(q_t, q_p, 0.25, 1.0, op0=A.mult, op1=A.subtract)
            e.tensor_tensor(q_tm, q_t, q_msk, op=A.mult)
            e.tensor_scalar(q_base, q_tm, 1.0, None, op0=A.add)
            e.reciprocal(q_rb, q_base)
            return e.tensor_tensor(q_f, q_base, half2d.to_broadcast([128, 32]), op=A.mult)
        sch.v(vstat, waits=sch.snap("s", "t", "g", "d"))

        def sabs2(e):
            e.activation(junk, x_ap, AF.Abs)
            return e.sign(xpb, x_ap)
        sch.s(sabs2, waits=sch.snap("v", "t", "g", "d"))

        def vaxs(e):
            return e.tensor_tensor(r3(qaxs), r3(junk), bc(q_rb), op=A.mult)
        sch.v(vaxs, waits=sch.snap("s"))

        def vA(e):
            e.tensor_scalar(qrA, qaxs, thA[0], None, op0=A.is_ge)
            for i in (1, 2, 3):
                e.scalar_tensor_tensor(qrA, qaxs, thA[i], qrA,
                                       op0=A.is_ge, op1=A.add)
            e.tensor_scalar(qrB, qaxs, thB[0], None, op0=A.is_ge)
            e.scalar_tensor_tensor(qrB, qaxs, thB[1], qrB, op0=A.is_ge, op1=A.add)
            e.tensor_scalar(qrC, qaxs, thC[0], None, op0=A.is_ge)
            return e.scalar_tensor_tensor(qrC, qrC, 2.0, qrB, op0=A.mult, op1=A.add)
        sch.v(vA, waits=sch.snap("t"))

        def vfin(e):
            e.scalar_tensor_tensor(qrC, qrC, 2.0, qrA, op0=A.mult, op1=A.add)
            e.tensor_tensor(r3(qrB), r3(xpb), bc(q_f), op=A.mult)
            return e.tensor_tensor(out_ap, qrC, qrB, op=A.mult)
        sch.v(vfin, waits=sch.snap("s"))

    def stream_w(src_ap, dc, buf, extra_waits=()):
        def wl(e):
            return e.dma_start(out=buf, in_=src_ap[dc * 128:(dc + 1) * 128, :])
        sch.dma(wl, waits=list(extra_waits), sem="w")

    wctr = [0]
    wbuf_tok = [None] * 5
    ps_tok = {"big": None, "med": None, "aux": None}

    def next_wbuf():
        slot = wctr[0] % 5
        wctr[0] += 1
        return wbuf[slot], slot

    front_tok = {}

    def p1_front_ln(i):
        p = i % 2
        row0 = i * 128

        def ldx(e):
            return e.dma_start(out=x_sb2[p], in_=x_in[row0:row0 + 128, :])
        sch.dma(ldx, waits=sch.snap("v", "s", "t", "g", "o"))
        dtok = sch.tok("d")
        emit_ln(x_sb2[p], xln2b[p], front=True, extra=[dtok])

    def p1_front_tr(i):
        p = i % 2
        w = [sch.tok("fv")]
        if ps_tok["aux"]:
            w.append(ps_tok["aux"])
        emit_transpose8(xln2b[p], ps_aux, xlnT2[p], front=True, extra=w)
        ps_tok["aux"] = sch.tok("fv")
        front_tok[i] = sch.tok("fv")

    def p1_back(i, next_tr=None):
        p = i % 2
        row0 = i * 128
        x_sb, xln, xlnT = x_sb2[p], xln2b[p], xlnT2[p]
        u_sb = x_sb
        qrB = p_sb[:, 0:D]
        qrC = p_sb[:, D:2 * D]

        def qkmm(e):
            last = None
            for oc in range(16):
                for dc in range(8):
                    last = e.matmul(ps_big[:, oc * 128:(oc + 1) * 128],
                                    wqk[dc][:, oc * 128:(oc + 1) * 128],
                                    xlnT[:, dc * 128:(dc + 1) * 128],
                                    start=(dc == 0), stop=(dc == 7))
            return last
        sch.t(qkmm, waits=[front_tok[i]] +
              [t for t in (ps_tok["big"],) if t] + sch.snap("d") if i == 0 else
              [front_tok[i]] + [t for t in (ps_tok["big"],) if t])

        for dc in range(8):
            wb, slot = next_wbuf()
            stream_w(wv_in, dc, wb,
                     extra_waits=[t for t in (wbuf_tok[slot],) if t])
            def vmm(e, dc=dc, wb=wb):
                last = None
                for ob in range(2):
                    last = e.matmul(ps_med[:, ob * 512:(ob + 1) * 512],
                                    xlnT[:, dc * 128:(dc + 1) * 128],
                                    wb[:, ob * 512:(ob + 1) * 512],
                                    start=(dc == 0), stop=False)
                return last
            w = [sch.tok("w")]
            if dc == 0 and ps_tok["med"]:
                w.append(ps_tok["med"])
            sch.t(vmm, waits=w)
            wbuf_tok[slot] = sch.tok("t")
        def vbias(e):
            e.matmul(ps_med[:, 0:512], ones1, bv_sb[:, 0:512], start=False, stop=True)
            return e.matmul(ps_med[:, 512:1024], ones1, bv_sb[:, 512:1024],
                            start=False, stop=True)
        sch.t(vbias)

        if next_tr is not None:
            next_tr()

        def evqk(e):
            for oc in range(16):
                e.raw.tensor_scalar(qkT[:, oc * 128:(oc + 1) * 128],
                                    ps_big[:, oc * 128:(oc + 1) * 128],
                                    bqk_sb[:, oc:oc + 1], None, op0=A.add)
            return e.tensor_copy(v_sb, ps_med)
        sch.v(evqk, waits=sch.snap("t"))
        ps_tok["med"] = sch.tok("v")

        def rebase(e):
            out = []
            for oc in range(16):
                out.append(e.dma_start(out=qkOdd[0:64, oc * 128:(oc + 1) * 128],
                                       in_=qkT[64:128, oc * 128:(oc + 1) * 128]))
            return out
        sch.dma(rebase, waits=sch.snap("v"), n=16)

        def qh(h):
            oc = h // 2
            src = qkT if h % 2 == 0 else qkOdd
            return src[0:64, oc * 128:(oc + 1) * 128]

        def kh(h):
            oc = 8 + h // 2
            src = qkT if h % 2 == 0 else qkOdd
            return src[0:64, oc * 128:(oc + 1) * 128]

        def smm(e):
            last = None
            for h in range(16):
                last = e.matmul(ps_big[:, h * 128:(h + 1) * 128], qh(h), kh(h),
                                start=True, stop=True)
            return last
        sch.t(smm, waits=sch.snap("d", "v"))

        def sm1(e):
            e.tensor_reduce(a_hm, ps_big.rearrange("p (h j) -> p h j", j=128),
                            mybir.AxisListType.X, A.max)
            e.reduce_max(a_cm, a_hm, mybir.AxisListType.X)
            return e.tensor_scalar(a_negc, a_cm, -0.125, None, op0=A.mult)
        sch.v(sm1, waits=sch.snap("t"))

        def sexp(e):
            return e.activation(p_sb, ps_big, AF.Exp, bias=a_negc, scale=c_0125)
        sch.s(sexp, waits=sch.snap("v"))

        def smask(e):
            e.tensor_tensor(p_sb.rearrange("p (h j) -> p h j", j=128),
                            p_sb.rearrange("p (h j) -> p h j", j=128),
                            mask.rearrange("p (o j) -> p o j", o=1)
                                .to_broadcast([128, 16, 128]),
                            op=A.mult)
            e.tensor_reduce(a_rs, p_sb.rearrange("p (h j) -> p h j", j=128),
                            mybir.AxisListType.X, A.add)
            return e.reciprocal(a_rr, a_rs)
        sch.v(smask, waits=sch.snap("s"))

        def ptr(e):
            last = None
            for h in range(16):
                last = e.transpose(ps_big[:, h * 128:(h + 1) * 128],
                                   p_sb[:, h * 128:(h + 1) * 128], ident)
            return last
        sch.t(ptr, waits=sch.snap("v"))

        def ptev(e):
            return e.tensor_copy(qkT, ps_big)
        sch.v(ptev, waits=sch.snap("t"))
        ps_tok["big"] = sch.tok("v")

        def avmm(e):
            last = None
            for h in range(16):
                last = e.matmul(ps_med[:, h * 64:(h + 1) * 64],
                                qkT[:, h * 128:(h + 1) * 128],
                                v_sb[:, h * 64:(h + 1) * 64],
                                start=True, stop=True)
            return last
        sch.t(avmm, waits=sch.snap("v"))

        def avev(e):
            return e.tensor_tensor(
                av_sb.rearrange("p (h j) -> p h j", j=64),
                ps_med.rearrange("p (h j) -> p h j", j=64),
                a_rr.rearrange("p (h o) -> p h o", o=1).to_broadcast([128, 16, 64]),
                op=A.mult)
        sch.v(avev, waits=sch.snap("t"))
        ps_tok["med"] = sch.tok("v")
        avev_tok = sch.tok("v")

        emit_transpose8(av_sb, ps_aux, avT,
                        extra=[ps_tok["aux"], avev_tok] if ps_tok["aux"]
                        else [avev_tok])
        ps_tok["aux"] = sch.tok("v")

        for dc in range(8):
            wb, slot = next_wbuf()
            stream_w(wpr_in, dc, wb,
                     extra_waits=[t for t in (wbuf_tok[slot],) if t])
            def pmm(e, dc=dc, wb=wb):
                last = None
                for ob in range(2):
                    last = e.matmul(ps_med[:, ob * 512:(ob + 1) * 512],
                                    avT[:, dc * 128:(dc + 1) * 128],
                                    wb[:, ob * 512:(ob + 1) * 512],
                                    start=(dc == 0), stop=False)
                return last
            w = [sch.tok("w"), sch.tok("v")] if dc == 0 else [sch.tok("w")]
            sch.t(pmm, waits=w)
            wbuf_tok[slot] = sch.tok("t")
        def pbias(e):
            e.matmul(ps_med[:, 0:512], ones1, bpr_sb[:, 0:512], start=False, stop=True)
            return e.matmul(ps_med[:, 512:1024], ones1, bpr_sb[:, 512:1024],
                            start=False, stop=True)
        sch.t(pbias)

        def resid1(e):
            return e.tensor_tensor(x1_sb, ps_med, x_sb, op=A.add)
        sch.v(resid1, waits=sch.snap("t"))
        ps_tok["med"] = sch.tok("v")

        emit_ln(x1_sb, xln)
        emit_quant_act(xln, u_sb, qrB, qrC)
        emit_transpose8(u_sb, ps_aux, uT,
                        extra=[ps_tok["aux"]] + sch.snap("v", "s"))
        ps_tok["aux"] = sch.tok("v")

        def fmm(e):
            last = None
            for ob in range(2):
                for dc in range(8):
                    last = e.matmul(ps_med[:, ob * 512:(ob + 1) * 512],
                                    uT[:, dc * 128:(dc + 1) * 128],
                                    wqt[dc][:, ob * 512:(ob + 1) * 512],
                                    start=(dc == 0), stop=False)
                last = e.matmul(ps_med[:, ob * 512:(ob + 1) * 512], ones1,
                                bffn_sb[:, ob * 512:(ob + 1) * 512],
                                start=False, stop=True)
            return last
        sch.t(fmm, waits=sch.snap("v"))

        def resid2(e):
            return e.tensor_tensor(xln, ps_med, x1_sb, op=A.add)
        sch.v(resid2, waits=sch.snap("t"))
        ps_tok["med"] = sch.tok("v")

        def stx2(e):
            return e.dma_start(out=x2d[row0:row0 + 128, :], in_=xln)
        sch.dma(stx2, waits=sch.snap("v"), sem="o")

        if upto == "p1":
            def sty(e):
                return e.dma_start(out=y_out[row0:row0 + 128, :], in_=xln)
            sch.dma(sty, waits=sch.snap("v"), sem="o")

    x2d_bm = x2d.rearrange("(s b) d -> b s d", b=B)

    def p2_front_ln(j):
        p = j % 2
        b, s0 = j // 8, (j % 8) * 128

        def ldx(e):
            return e.dma_start(out=x_sb2[p], in_=x2d_bm[b, s0:s0 + 128, :])
        sch.dma(ldx, waits=sch.snap("v", "s", "t", "g", "o"))
        dtok = sch.tok("d")
        emit_ln(x_sb2[p], xln2b[p], front=True, extra=[dtok])

    def p2_front_tr(j):
        p = j % 2
        w = [sch.tok("fv")]
        if ps_tok["aux"]:
            w.append(ps_tok["aux"])
        emit_transpose8(xln2b[p], ps_aux, xlnT2[p], front=True, extra=w)
        ps_tok["aux"] = sch.tok("fv")
        front_tok[j] = sch.tok("fv")

    def p2_back(j, next_tr=None):
        p = j % 2
        b = j // 8
        xlnT = xlnT2[p]
        k_ps = ps_big[:, 0:1024]
        v_ps = ps_big[:, 1024:2048]

        first = True
        for dc in range(8):
            bufs = []
            slots = []
            for part in range(3):
                wb, slot = next_wbuf()
                bufs.append(wb)
                slots.append(slot)
                stream_w(wqb_in.rearrange("d (pp o) -> pp d o", pp=3)[part], dc,
                         wb, extra_waits=[t for t in (wbuf_tok[slot],) if t])
            def qmm(e, dc=dc, bufs=bufs):
                tgts = [ps_med, k_ps, v_ps]
                last = None
                for part in range(3):
                    for ob in range(2):
                        last = e.matmul(tgts[part][:, ob * 512:(ob + 1) * 512],
                                        xlnT[:, dc * 128:(dc + 1) * 128],
                                        bufs[part][:, ob * 512:(ob + 1) * 512],
                                        start=(dc == 0), stop=(dc == 7))
                return last
            w = [sch.tok("w")]
            if first:
                w = w + [front_tok[j]] + sch.snap("v")
                first = False
            sch.t(qmm, waits=w)
            tk = sch.tok("t")
            for slot in slots:
                wbuf_tok[slot] = tk

        if next_tr is not None:
            next_tr()

        def rms(ps, wrep, dst):
            def s1(e):
                return e.activation(junk, ps, AF.Square, accum_out=r_ssq)
            sch.s(s1, waits=sch.snap("t", "v"))
            def v1(e):
                e.tensor_scalar(r_t0, r_ssq, 1.0 / D, RMS_EPS, op0=A.mult, op1=A.add)
                return e.reciprocal(r_t1, r_t0)
            sch.v(v1, waits=sch.snap("s"))
            def s2(e):
                return e.activation(r_r0, r_t1, AF.Sqrt)
            sch.s(s2, waits=sch.snap("v"))
            def v2(e):
                e.tensor_tensor(r_rq, r_r0, r_r0, op=A.mult)
                e.tensor_tensor(r_rq, r_rq, r_t0, op=A.mult)
                e.tensor_scalar(r_rq, r_rq, -0.5, 1.5, op0=A.mult, op1=A.add)
                e.tensor_tensor(r_rq, r_r0, r_rq, op=A.mult)
                e.tensor_scalar(junk, ps, r_rq, None, op0=A.mult)
                e.tensor_tensor(junk, junk, wrep, op=A.mult)
                return e.tensor_scalar(dst, junk, 0.0, None, op0=A.max)
            sch.v(v2, waits=sch.snap("s"))

        qn_ap = qkT[:, 0:1024]
        kn_ap = qkT[:, 1024:2048]
        rms(ps_med, qnw_sb, qn_ap)
        rms(k_ps, knw_sb, kn_ap)
        def vev(e):
            return e.tensor_copy(v_sb, v_ps)
        sch.v(vev, waits=sch.snap("t"))

        def kvmm(e):
            last = None
            for h in range(16):
                last = e.matmul(ps_med[0:64, h * 64:(h + 1) * 64],
                                kn_ap[:, h * 64:(h + 1) * 64],
                                v_sb[:, h * 64:(h + 1) * 64],
                                start=True, stop=True)
            for h in range(16):
                last = e.matmul(ps_big[0:64, h:h + 1],
                                kn_ap[:, h * 64:(h + 1) * 64], ones128,
                                start=True, stop=True)
            return last
        sch.t(kvmm, waits=sch.snap("v"))

        def kvacc(e):
            e.raw.tensor_tensor(kvacc_ap(b), ps_med[0:64, 0:1024],
                                kvacc_ap(b), op=A.add)
            return e.tensor_tensor(ksacc_ap(b), ps_big[0:64, 0:16],
                                   ksacc_ap(b), op=A.add)
        sch.v(kvacc, waits=sch.snap("t"))

        def stqn(e):
            return e.dma_start(out=qnd[j * 128:(j + 1) * 128, :], in_=qn_ap)
        sch.dma(stqn, waits=sch.snap("v"), sem="o")

    def p3():
        def st(e):
            out = []
            for bb in range(B):
                out.append(e.dma_start(out=cc_in.ap()[:, bb * 1024:(bb + 1) * 1024],
                                       in_=kvacc_ap(bb)))
            out.append(e.dma_start(out=cc_in.ap()[:, 4096:4160],
                                   in_=wqk[2][0:64, 0:64]))
            return out
        sch.dma(st, waits=sch.snap("v"), n=5)

        def cc(e):
            return e.collective_compute(
                "AllReduce", A.add, replica_groups=[list(range(NCORES))],
                ins=[cc_in.ap().opt()], outs=[cc_out.ap().opt()])
        sch.emit("gpsimd", cc, waits=sch.snap("d"), inc=scc, n=1, amt=1)

        def ld(e):
            o1 = e.dma_start(out=wqk[3][0:64, :], in_=cc_out.ap()[:, 0:2048])
            o2 = e.dma_start(out=wqk[4][0:64, :], in_=cc_out.ap()[:, 2048:4096])
            o3 = e.dma_start(out=wqk[5][0:64, 0:64], in_=cc_out.ap()[:, 4096:4160])
            return [o1, o2, o3]
        sch.dma(ld, waits=sch.snap("c"), n=3)

    def p4_front_ln(j):
        p = j % 2
        b, s0 = j // 8, (j % 8) * 128
        qn_ap = p_sb[:, p * 1024:(p + 1) * 1024]

        def ldq(e):
            o1 = e.dma_start(out=qn_ap, in_=qnd[j * 128:(j + 1) * 128, :])
            o2 = e.dma_start(out=x_sb2[p], in_=x2d_bm[b, s0:s0 + 128, :])
            return [o1, o2]
        sch.dma(ldq, waits=sch.snap("v", "s", "t", "g", "o"), n=2)
        front_tok[(j, "d")] = sch.tok("d")

    def p4_front_tr(j):
        p = j % 2
        qn_ap = p_sb[:, p * 1024:(p + 1) * 1024]
        w = [sch.tok("fv"), front_tok[(j, "d")]]
        if ps_tok["aux"]:
            w.append(ps_tok["aux"])
        emit_transpose8(qn_ap, ps_aux, xlnT2[p], front=True, extra=w)
        ps_tok["aux"] = sch.tok("fv")
        evtok = sch.tok("fv")

        def rebase(e):
            out = []
            for dc in range(8):
                out.append(e.dma_start(
                    out=qkOdd[0:64, p * 1024 + dc * 128:p * 1024 + (dc + 1) * 128],
                    in_=xlnT2[p][64:128, dc * 128:(dc + 1) * 128]))
            return out
        sch.dma(rebase, waits=[evtok] + sch.snap("t"), n=8)
        front_tok[j] = (evtok, sch.tok("d"))

    def p4_back(j, next_tr=None):
        p = j % 2
        b = j // 8
        x_sb, xln, xlnT = x_sb2[p], xln2b[p], xlnT2[p]
        u_sb = x_sb
        qrB = qkT[:, 0:D]
        qrC = qkT[:, D:2 * D]

        def qh(h):
            dc = h // 2
            if h % 2 == 0:
                return xlnT[0:64, dc * 128:(dc + 1) * 128]
            return qkOdd[0:64, p * 1024 + dc * 128:p * 1024 + (dc + 1) * 128]

        evtok, rbtok = front_tok[j]

        def attmm(e):
            last = None
            for h in range(16):
                last = e.matmul(ps_med[:, h * 64:(h + 1) * 64], qh(h),
                                kvg_kv(b, h), start=True, stop=True)
            for h in range(16):
                last = e.matmul(ps_aux[:, h:h + 1], qh(h),
                                kvg_ks(b, h), start=True, stop=True)
            return last
        aw = [evtok, rbtok, ps_tok["aux"]]
        if ps_tok["med"]:
            aw.append(ps_tok["med"])
        sch.t(attmm, waits=aw + (sch.snap("v") if b == 0 and j % 8 == 0 else []))

        def nrm(e):
            e.tensor_scalar(nrm_sb, ps_aux[:, 0:16], DEN_EPS, None, op0=A.add)
            e.reciprocal(nrr_sb, nrm_sb)
            return e.tensor_tensor(
                av_sb.rearrange("p (h j) -> p h j", j=64),
                ps_med.rearrange("p (h j) -> p h j", j=64),
                nrr_sb.rearrange("p (h o) -> p h o", o=1).to_broadcast([128, 16, 64]),
                op=A.mult)
        sch.v(nrm, waits=sch.snap("t"))
        nrm_tok = sch.tok("v")

        emit_transpose8(av_sb, ps_aux, avT, extra=[nrm_tok])
        ps_tok["aux"] = sch.tok("v")

        for dc in range(8):
            wb, slot = next_wbuf()
            stream_w(wout_in, dc, wb,
                     extra_waits=[t for t in (wbuf_tok[slot],) if t])
            def omm(e, dc=dc, wb=wb):
                last = None
                for ob in range(2):
                    last = e.matmul(ps_med[:, ob * 512:(ob + 1) * 512],
                                    avT[:, dc * 128:(dc + 1) * 128],
                                    wb[:, ob * 512:(ob + 1) * 512],
                                    start=(dc == 0), stop=False)
                return last
            w = [sch.tok("w"), sch.tok("v")] if dc == 0 else [sch.tok("w")]
            sch.t(omm, waits=w)
            wbuf_tok[slot] = sch.tok("t")
        def obias(e):
            e.matmul(ps_med[:, 0:512], ones1, bo_sb[:, 0:512], start=False, stop=True)
            return e.matmul(ps_med[:, 512:1024], ones1, bo_sb[:, 512:1024],
                            start=False, stop=True)
        sch.t(obias)

        if next_tr is not None:
            next_tr()

        def resid3(e):
            return e.tensor_tensor(x1_sb, ps_med, x_sb, op=A.add)
        sch.v(resid3, waits=sch.snap("t"))
        ps_tok["med"] = sch.tok("v")

        emit_ln(x1_sb, xln)
        emit_quant_act(xln, u_sb, qrB, qrC)
        emit_transpose8(u_sb, ps_aux, uT,
                        extra=[ps_tok["aux"]] + sch.snap("v", "s"))
        ps_tok["aux"] = sch.tok("v")

        def fmm(e):
            last = None
            for ob in range(2):
                for dc in range(8):
                    last = e.matmul(ps_big[:, ob * 512:(ob + 1) * 512],
                                    uT[:, dc * 128:(dc + 1) * 128],
                                    wqt[dc][:, ob * 512:(ob + 1) * 512],
                                    start=(dc == 0), stop=False)
                last = e.matmul(ps_big[:, ob * 512:(ob + 1) * 512], ones1,
                                bffn_sb[:, ob * 512:(ob + 1) * 512],
                                start=False, stop=True)
            return last
        fw = sch.snap("v")
        if ps_tok["big"]:
            fw = fw + [ps_tok["big"]]
        sch.t(fmm, waits=fw)

        def resid4(e):
            return e.tensor_tensor(xln, ps_big[:, 0:D], x1_sb, op=A.add)
        sch.v(resid4, waits=sch.snap("t"))
        ps_tok["big"] = sch.tok("v")

        def sty(e):
            return e.dma_start(out=y_out[j * 128:(j + 1) * 128, :], in_=xln)
        sch.dma(sty, waits=sch.snap("v"), sem="o")

    p0()
    p1_front_ln(0)
    p1_front_tr(0)
    for i in range(nt):
        if i + 1 < nt:
            p1_front_ln(i + 1)
            p1_back(i, next_tr=lambda i=i: p1_front_tr(i + 1))
        else:
            p1_back(i)

    if upto != "p1":
        def zacc(e):
            for bb in range(B):
                e.memset(kvacc_ap(bb), 0.0)
                e.memset(ksacc_ap(bb), 0.0)
            return e.memset(wqk[2][0:64, 0:64], 0.0)
        sch.g(zacc, waits=sch.snap("v", "t"))
        p2_front_ln(0)
        p2_front_tr(0)
        for j in range(nt):
            if j + 1 < nt:
                p2_front_ln(j + 1)
                p2_back(j, next_tr=lambda j=j: p2_front_tr(j + 1))
            else:
                p2_back(j)
        if upto not in ("p2", "p3"):
            p3()
            p4_front_ln(0)
            p4_front_tr(0)
            for j in range(nt):
                if j + 1 < nt:
                    p4_front_ln(j + 1)
                    p4_back(j, next_tr=lambda j=j: p4_front_tr(j + 1))
                else:
                    p4_back(j)
        elif upto == "p3":
            p3()

    o_final = so.n

    def fin(e):
        e.wait_ge(so.h, o_final)
        return None
    sch.emit("sync", fin, inc=None)

    with nc.Block() as block:
        @block.sync
        def _(e):
            for fn in sch.q["sync"]:
                fn(e)

        @block.vector
        def _(e):
            e = _Drainer(e)
            for fn in sch.q["vector"]:
                fn(e)

        @block.scalar
        def _(e):
            e = _Drainer(e)
            for fn in sch.q["scalar"]:
                fn(e)

        @block.tensor
        def _(e):
            for fn in sch.q["tensor"]:
                fn(e)

        @block.gpsimd
        def _(e):
            e = _Drainer(e)
            for fn in sch.q["gpsimd"]:
                fn(e)

    stack.close()
    return nc


def stage_inputs(inputs, core):
    x = inputs["L0_attn_a_x"]
    xs = x[:, core * S_CORE:(core + 1) * S_CORE, :]
    x_core = np.ascontiguousarray(np.transpose(xs, (1, 0, 2)).reshape(T_CORE, D))

    wqkv = inputs["L0_attn_a_Wqkv"]
    bqkv = inputs["L0_attn_a_bqkv"]
    dw = inputs["L0_ffn_delta_w"]

    return {
        "x": x_core,
        "wqk_t": np.ascontiguousarray(wqkv[0:2048, :].T),
        "bqk_pp": np.ascontiguousarray(bqkv[0:2048].reshape(16, 128).T),
        "wv_t": np.ascontiguousarray(wqkv[2048:3072, :].T),
        "bv_row": np.ascontiguousarray(bqkv[2048:3072][None, :]),
        "wproj_t": np.ascontiguousarray(inputs["L0_attn_a_Wproj"].T),
        "bproj_row": np.ascontiguousarray(inputs["L0_attn_a_bproj"][None, :]),
        "wffn": np.ascontiguousarray(inputs["L0_ffn_weight"]),
        "dw": np.ascontiguousarray(
            np.transpose(dw.reshape(8, 128, 32), (1, 0, 2)).reshape(128, 256)),
        "bffn_row": np.ascontiguousarray(inputs["L0_ffn_bias"][None, :]),
        "dact": np.ascontiguousarray(np.broadcast_to(
            inputs["L0_ffn_delta_act"].reshape(1, 1), (128, 1))),
        "wqkv_b": np.ascontiguousarray(inputs["L0_attn_b_W_qkv"]),
        "qnw_rep": np.ascontiguousarray(
            np.broadcast_to(inputs["L0_attn_b_qn_w"][None, :], (128, D))),
        "knw_rep": np.ascontiguousarray(
            np.broadcast_to(inputs["L0_attn_b_kn_w"][None, :], (128, D))),
        "wout": np.ascontiguousarray(inputs["L0_attn_b_W_out"]),
        "bout_row": np.ascontiguousarray(inputs["L0_attn_b_b_out"][None, :]),
        "ident": np.eye(128, dtype=np.float32),
        "mask": np.kron(np.eye(32, dtype=np.float32),
                        np.ones((4, 4), dtype=np.float32)),
    }


def kernel(**inputs):
    inputs = {k: np.asarray(v, dtype=np.float32) for k, v in inputs.items()}
    nc = build_program("all")
    in_maps = [stage_inputs(inputs, c) for c in range(NCORES)]
    res = run_bass_kernel_spmd(nc, in_maps, core_ids=list(range(NCORES)))
    out = np.empty((B, S, D), dtype=np.float32)
    for c in range(NCORES):
        yc = np.asarray(res.results[c]["y"], dtype=np.float32).reshape(B, S_CORE, D)
        out[:, c * S_CORE:(c + 1) * S_CORE, :] = yc
    return out
